# version 26
# baseline (speedup 1.0000x reference)
import numpy as np
import ml_dtypes
import concourse.bass as bass
import concourse.mybir as mybir
from concourse.bass_utils import run_bass_kernel_spmd

F32 = mybir.dt.float32
F32R = mybir.dt.float32r
BF16 = mybir.dt.bfloat16
U8 = mybir.dt.uint8
AF = mybir.ActivationFunctionType
ALU = mybir.AluOpType
AX = mybir.AxisListType.X

D = 1024
NH = 16
HD = 64
CH = 128
GRP = 256
HID = 4096
SEQ = 2048
NSEQ = 2
NGRP = SEQ // GRP
MASK_RAW = -262144.0
EXP_SHIFT = -12.0

ENG = ("T", "V", "S", "Q0", "Q1")
NPOOL = {"Q0": 12, "Q1": 8, "Q2": 8}
DIMS = (["T", "V", "S", "G"]
        + [(q, k) for q in ("Q0", "Q1", "Q2") for k in range(NPOOL[q])])


class Prog:

    def __init__(self, nc):
        self.nc = nc
        self.thunks = {e: [] for e in ENG}
        self.cnt = {e: 0 for e in ("T", "V", "S", "G")}
        self.dcnt = {q: 0 for q in ("Q0", "Q1", "Q2")}
        self.vc = {e: {d: 0 for d in DIMS} for e in ENG}
        self.snap = {d: [None] for d in DIMS}
        self.bufs = {}
        self.sems = {}
        self.pool_uses = {q: [0] * NPOOL[q] for q in ("Q0", "Q1", "Q2")}
        self.pend = {}

    def make_sems(self, estack):
        for e in ("T", "V", "S", "G"):
            self.sems[e] = estack.enter_context(self.nc.semaphore("sem" + e))
        for q in ("Q0", "Q1", "Q2"):
            for k in range(NPOOL[q]):
                self.sems[(q, k)] = estack.enter_context(
                    self.nc.semaphore("sem%s_%d" % (q, k)))

    def _need_for(self, reads, writes):
        need = {}
        for key in reads:
            b = self.bufs.get(key)
            if b and b[0] is not None:
                e, i = b[0]
                if i > need.get(e, 0):
                    need[e] = i
        for key in writes:
            b = self.bufs.get(key)
            if b:
                if b[0] is not None:
                    e, i = b[0]
                    if i > need.get(e, 0):
                        need[e] = i
                for e, i in b[1].items():
                    if i > need.get(e, 0):
                        need[e] = i
        return need

    def _emit_waits(self, eng, need):
        myvc = self.vc[eng]
        for d in sorted(need, key=str):
            i = need[d]
            if myvc[d] >= i:
                continue
            if isinstance(d, tuple):
                self.thunks[eng].append(("w", self.sems[d], 16 * i))
            else:
                self.thunks[eng].append(("w", self.sems[d], i))
            sn = self.snap[d][i]
            for dd, v in sn.items():
                if v > myvc[dd]:
                    myvc[dd] = v
            if i > myvc[d]:
                myvc[d] = i

    def _marker(self, eng):
        self.cnt[eng] += 1
        i = self.cnt[eng]
        sn = dict(self.vc[eng])
        sn[eng] = i
        self.snap[eng].append(sn)
        return (eng, i)

    def _register(self, reads, writes, marker):
        for key in writes:
            self.bufs[key] = [marker, {}]
        e, i = marker
        for key in reads:
            b = self.bufs.get(key)
            if b is None:
                b = [None, {}]
                self.bufs[key] = b
            if i > b[1].get(e, 0):
                b[1][e] = i

    def op(self, eng, emit, reads=(), writes=()):
        stream = "Q1" if eng == "G" else eng
        self._emit_waits(stream, self._need_for(reads, writes))
        self.cnt[eng] += 1
        i = self.cnt[eng]
        sn = dict(self.vc[stream])
        sn[eng] = i
        self.snap[eng].append(sn)
        self.thunks[stream].append(("i", emit, self.sems[eng], 1))
        self._register(reads, writes, (eng, i))

    def gop(self, eng, emit, reads=(), gkey=None, first=False, close=False,
            final=False):
        writes = (gkey,) if first else ()
        self._emit_waits(eng, self._need_for(reads, writes))
        pend = self.pend.setdefault((eng, gkey), set())
        pend.update(reads)
        if close:
            m = self._marker(eng)
            self.thunks[eng].append(("i", emit, self.sems[eng], 1))
            self._register(tuple(pend), (gkey,) if final else (), m)
            pend.clear()
            if final:
                self.pend.pop((eng, gkey), None)
        else:
            self.thunks[eng].append(("x", emit))

    def dma(self, q, out_ap, in_ap, reads=(), writes=()):
        stream = "S" if q == "Q2" else q
        R = NPOOL[q]
        idx = self.dcnt[q]
        self.dcnt[q] = idx + 1
        k = idx % R
        d = (q, k)
        prev_uses = self.pool_uses[q][k]
        if prev_uses > 0 and self.vc[stream][d] < prev_uses:
            self.thunks[stream].append(("w", self.sems[d], 16 * prev_uses))
            sn = self.snap[d][prev_uses]
            myvc = self.vc[stream]
            for dd, v in sn.items():
                if v > myvc[dd]:
                    myvc[dd] = v
        self._emit_waits(stream, self._need_for(reads, writes))
        self.thunks[stream].append(("d", out_ap, in_ap, self.sems[d]))
        u = prev_uses + 1
        self.pool_uses[q][k] = u
        sn = dict(self.vc[stream])
        sn[d] = u
        self.snap[d].append(sn)
        self._register(reads, writes, (d, u))

    def finalize_queues(self):
        for q in ("Q0", "Q1", "Q2"):
            stream = "S" if q == "Q2" else q
            for k in range(NPOOL[q]):
                u = self.pool_uses[q][k]
                if u > 0:
                    self.thunks[stream].append(("w", self.sems[(q, k)], 16 * u))

    def render(self):
        nc = self.nc

        def runner(e):
            lst = self.thunks[e]
            def run(engine_obj):
                for t in lst:
                    tag = t[0]
                    if tag == "w":
                        engine_obj.wait_ge(t[1], t[2])
                    elif tag == "i":
                        t[1](engine_obj).then_inc(t[2], t[3])
                    elif tag == "x":
                        t[1](engine_obj)
                    else:
                        engine_obj.dma_start(out=t[1], in_=t[2]).then_inc(t[3], 16)
            return run

        with nc.Block() as block:
            block.tensor(runner("T"))
            block.vector(runner("V"))
            block.scalar(runner("S"))
            block.sync(runner("Q0"))
            block.gpsimd(runner("Q1"))



def build_program(n_seq=NSEQ, n_grp=NGRP):
    import contextlib
    nc = bass.Bass("TRN2", target_bir_lowering=False, debug=False)

    x_d = nc.dram_tensor("x", [NSEQ * SEQ, D], F32, kind="ExternalInput").ap()
    y_d = nc.dram_tensor("y", [NSEQ * SEQ, D], F32, kind="ExternalOutput").ap()
    wt_d = {}
    for nm, t in (("wq", "wqt"), ("wk", "wkt"), ("wv", "wvt"), ("wo", "wot")):
        wt_d[nm] = nc.dram_tensor(t, [D, D], F32R, kind="ExternalInput").ap()
    k0s_d = nc.dram_tensor("k0s", [HID, 2 * D], U8, kind="ExternalInput").ap()
    k1s_d = nc.dram_tensor("k1s", [HID, HID], U8, kind="ExternalInput").ap()
    k2_d = nc.dram_tensor("k2", [HID, D], U8, kind="ExternalInput").ap()
    scal_d = nc.dram_tensor("scal", [128, 4], F32, kind="ExternalInput").ap()
    cmsk_d = nc.dram_tensor("cmsk", [128, 256], F32R, kind="ExternalInput").ap()
    cmsk0_d = nc.dram_tensor("cmsk0", [128, 256], F32R,
                             kind="ExternalInput").ap()
    cid_d = nc.dram_tensor("cid", [128, 128], BF16, kind="ExternalInput").ap()
    cidr_d = nc.dram_tensor("cidr", [128, 128], F32R, kind="ExternalInput").ap()
    cn12_d = nc.dram_tensor("cn12", [128, 1], F32, kind="ExternalInput").ap()
    czero_d = nc.dram_tensor("czero", [128, 8 * 384], F32R,
                             kind="ExternalInput").ap()
    d0s_d = nc.dram_tensor("d0s", [HID, D], F32R, kind="Internal").ap()
    w1pm_d = nc.dram_tensor("w1pm", [HID, HID], F32R, kind="Internal").ap()
    w2pm_d = nc.dram_tensor("w2pm", [HID, D], F32R, kind="Internal").ap()

    estack = contextlib.ExitStack()
    with estack:
        A = nc.alloc_sbuf_tensor
        s_cid = A("s_cid", [128, 128], BF16)
        s_cidr = A("s_cidr", [128, 128], F32R)
        s_msk = A("s_msk", [128, 256], F32R)
        s_msk0 = A("s_msk0", [128, 256], F32R)
        s_scal = A("s_scal", [128, 4], F32)
        s_n12 = A("s_n12", [128, 1], F32)
        s_S0 = A("s_S0", [128, 1], F32)
        s_S1 = A("s_S1", [128, 1], F32)
        s_S2 = A("s_S2", [128, 1], F32)
        s_ln = A("s_ln", [128, 16], F32)
        s_sums = A("s_sums", [128, NH], F32)
        s_rcp = A("s_rcp", [128, NH], F32)
        s_pre = A("s_pre", [128, 1024], BF16)
        s_prer = A("s_prer", [128, 1024], F32R)
        s_xc = A("s_xc", [128, 6 * D], F32)
        s_xnt = {p: A("s_xnt_" + p, [128, D], BF16) for p in ("hi", "lo")}
        s_xn = A("s_xn", [128, 8 * 512], F32R)
        s_q = A("s_q", [128, 8 * GRP], F32R)
        s_k = A("s_k", [128, 8 * 384], F32R)
        s_v = {p: A("s_v_" + p, [128, 3 * D], BF16) for p in ("hi", "lo")}
        s_O = A("s_O", [128, 8 * GRP], F32R)
        s_P0 = A("s_P0", [128, 2 * 256], F32)
        s_Ph = A("s_Ph", [128, 2 * 256], BF16)
        s_Pl = A("s_Pl", [128, 2 * 256], BF16)
        s_PT = A("s_PT", [128, 2 * 512], BF16)
        s_wbuf = [A("s_wbuf%d" % b, [128, 512], F32R) for b in range(4)]
        s_d0b = [A("s_d0b%d" % b, [128, 1024], F32R) for b in range(3)]
        s_k1b = [A("s_k1b%d" % b, [128, 1024], F32R) for b in range(3)]
        s_w2b = [A("s_w2b%d" % b, [128, D], F32R) for b in range(2)]
        s_a0 = A("s_a0", [128, 32 * 512], F32R)
        s_a1 = A("s_a1", [128, 2 * 512], F32R)

        ps = [nc.alloc_psum_tensor("ps%d" % i, [128, 512], F32) for i in range(6)]
        psb = [nc.alloc_psum_tensor("pb%d" % i, [128, 1024], BF16) for i in range(2)]

        pr = Prog(nc)
        pr.make_sems(estack)

        def ts(eng, out_ap, in_ap, s1, s2, op0, op1, reads, writes):
            def emit(e, o=out_ap, i=in_ap, s1=s1, s2=s2, op0=op0, op1=op1):
                return e.tensor_scalar(out=o, in0=i, scalar1=s1, scalar2=s2,
                                       op0=op0, op1=op1)
            pr.op(eng, emit, reads=reads, writes=writes)

        def stt(eng, out_ap, in0_ap, sc, in1_ap, op0, op1, reads, writes):
            def emit(e, o=out_ap, i0=in0_ap, sc=sc, i1=in1_ap, op0=op0, op1=op1):
                return e.scalar_tensor_tensor(out=o, in0=i0, scalar=sc, in1=i1,
                                              op0=op0, op1=op1)
            pr.op(eng, emit, reads=reads, writes=writes)

        def tt(eng, out_ap, a_ap, b_ap, alu, reads, writes):
            def emit(e, o=out_ap, a=a_ap, b=b_ap, alu=alu):
                return e.tensor_tensor(out=o, in0=a, in1=b, op=alu)
            pr.op(eng, emit, reads=reads, writes=writes)

        def cp(eng, out_ap, in_ap, reads, writes):
            if eng == "S":
                def emit(e, o=out_ap, i=in_ap):
                    return e.copy(out=o, in_=i)
            else:
                def emit(e, o=out_ap, i=in_ap):
                    return e.tensor_copy(out=o, in_=i)
            pr.op(eng, emit, reads=reads, writes=writes)

        def memset(eng, ap, val, reads, writes):
            def emit(e, a=ap, v=val):
                return e.memset(a, v)
            pr.op(eng, emit, reads=reads, writes=writes)

        def act(out_ap, in_ap, func, bias, scale, accum, reads, writes):
            def emit(e, o=out_ap, i=in_ap, f=func, b=bias, s=scale, a=accum):
                return e.activation(out=o, in_=i, func=f, bias=b, scale=s,
                                    accum_out=a)
            pr.op("S", emit, reads=reads, writes=writes)

        def mm(gkey, out_ap, lhsT, rhs, reads, start, stop, close, final, tp=None):
            def emit(e, o=out_ap, l=lhsT, r=rhs, st=start, sp=stop, tp=tp):
                return e.matmul(o, l, r, start=st, stop=sp, tile_position=tp)
            pr.gop("T", emit, reads=reads, gkey=gkey, first=start, close=close,
                   final=final)

        def tr(gkey, out_ap, in_ap, reads, first, close, final):
            def emit(e, o=out_ap, i=in_ap):
                return e.transpose(o, i, s_cid[:])
            pr.gop("T", emit, reads=tuple(reads) + (("cid",),), gkey=gkey,
                   first=first, close=close, final=final)

        groups = [(sq, g) for sq in range(n_seq) for g in range(n_grp)]
        assert len(groups) % 2 == 0
        npairs = len(groups) // 2

        def xnk(pos):
            return [("xn", pos, 0), ("xn", pos, 1)]
        xn_all = xnk(0) + xnk(1)
        a0keys = [("a0", m) for m in range(32)]
        r1024 = 1.0 / 1024.0

        pf_list = []
        for _pj in range(npairs):
            for _g in range(2):
                for _nm in ("wq", "wk", "wv", "wo"):
                    for _ic in range(8):
                        for _hf in range(2):
                            pf_list.append((_nm, _ic, _hf))
        pf_state = [0, 0]

        def pf_issue(n):
            for _ in range(n):
                i = pf_state[0]
                if i >= len(pf_list) or i - pf_state[1] >= 4:
                    return
                nm, ic, hf = pf_list[i]
                b = i % 4
                pr.dma("Q0", s_wbuf[b][:],
                       wt_d[nm][ic * 128:(ic + 1) * 128,
                                hf * 512:(hf + 1) * 512],
                       reads=[("wbuf", b)], writes=[("wbuf", b)])
                pf_state[0] = i + 1

        def panel_use(nm, ic, hf):
            i = pf_state[1]
            assert pf_list[i] == (nm, ic, hf), (pf_list[i], nm, ic, hf)
            assert i < pf_state[0]
            pf_state[1] = i + 1
            return i % 4

        def pre_consts():
            pr.dma("Q0", s_cid[:], cid_d, writes=[("cid",)])
            pr.dma("Q0", s_cidr[:], cidr_d, writes=[("cidr",)])
            pr.dma("Q0", s_msk[:], cmsk_d, writes=[("msk",)])
            pr.dma("Q0", s_msk0[:], cmsk0_d, writes=[("msk0",)])
            pr.dma("Q0", s_scal[:], scal_d, writes=[("scal",)])
            pr.dma("Q0", s_n12[:], cn12_d, writes=[("n12",)])
            ts("V", s_S0[:], s_scal[:, 0:1], 4.0, None, ALU.mult, ALU.bypass,
               reads=[("scal",)], writes=[("S0",)])
            ts("V", s_S1[:], s_scal[:, 1:2], 2.0, None, ALU.mult, ALU.bypass,
               reads=[("scal",)], writes=[("S1",)])
            ts("V", s_S2[:], s_scal[:, 2:3], 1.0, None, ALU.mult, ALU.bypass,
               reads=[("scal",)], writes=[("S2",)])
            pr.dma("Q0", s_k[:], czero_d, reads=[],
                   writes=[("kcar",)] + [("k", b) for b in range(4)])
            for p in ("hi", "lo"):
                memset("V", s_v[p][:, 2 * D:3 * D], 0.0, reads=[],
                       writes=[("v", p, 2, 0), ("v", p, 2, 1)])

        pre_tasks = []

        tsk_n = [0]

        def task_d0(m, qh):
            def run():
                kt = ("pre", 0)
                kb = ("pre", 1)
                kr = ("prer", tsk_n[0] % 2)
                bt = s_pre[:, 0:512]
                bb = s_pre[:, 512:1024]
                br = s_prer[:, (tsk_n[0] % 2) * 512:(tsk_n[0] % 2) * 512 + 512]
                r0 = m * 128
                pr.dma("Q1", bt, k0s_d[r0:r0 + 128, qh * 512:qh * 512 + 512],
                       reads=[kt], writes=[kt])
                pr.dma("Q1", bb, k0s_d[r0:r0 + 128,
                                       1024 + qh * 512:1024 + qh * 512 + 512],
                       reads=[kb], writes=[kb])
                tt("V", br, bt, bb, ALU.subtract, reads=[kt, kb, kr],
                   writes=[kr])
                pr.dma("Q0", d0s_d[r0:r0 + 128, qh * 512:(qh + 1) * 512], br,
                       reads=[kr], writes=[("d0sd", m, qh)])
                tsk_n[0] += 1
            return run

        def task_pm(src_d, dst_d, rc, c0, wkey):
            def run():
                h = tsk_n[0] % 2
                kk = ("pre", h)
                kr = ("prer", h)
                buf = s_pre[:, h * 512:(h + 1) * 512]
                bufr = s_prer[:, h * 512:(h + 1) * 512]
                pr.dma("Q1", buf, src_d[rc:rc + 128, c0:c0 + 512], reads=[kk],
                       writes=[kk])
                if h == 0:
                    act(bufr, buf, AF.Copy, -1.0, 2.0, None,
                        reads=[kk, kr], writes=[kr])
                else:
                    ts("V", bufr, buf, 2.0, -1.0, ALU.mult, ALU.add,
                       reads=[kk, kr], writes=[kr])
                pr.dma("Q0", dst_d[rc:rc + 128, c0:c0 + 512], bufr,
                       reads=[kr], writes=[wkey])
                tsk_n[0] += 1
            return run

        def build_pre_tasks():
            for m in range(32):
                for qh in range(2):
                    pre_tasks.append(task_d0(m, qh))
            for oc in range(32):
                for hf2 in range(2):
                    pre_tasks.append(task_pm(k2_d, w2pm_d, oc * 128,
                                             hf2 * 512, ("w2pmd", oc, hf2)))
            for m2 in range(32):
                for e in range(8):
                    pre_tasks.append(task_pm(k1s_d, w1pm_d, m2 * 128,
                                             e * 512, ("w1pmd", m2, e)))

        def drain(n):
            for _ in range(n):
                if pre_tasks:
                    pre_tasks.pop(0)()

        def phase_X_ln(gi, cc):
            sq, g = groups[gi]
            gch = 2 * gi + cc
            slot = gch % 6
            xt = s_xc[:, slot * D:(slot + 1) * D]
            kx = ("xc", slot)
            r0 = sq * SEQ + g * GRP + cc * 128
            L = lambda i, cc=cc: ("ln", cc, i)
            lc = s_ln[:, cc * 8:(cc + 1) * 8]
            pr.dma("Q0", xt, x_d[r0:r0 + 128, :], reads=[kx], writes=[kx])
            def em_rs(e, o=lc[:, 0:1], i=xt):
                return e.reduce_sum(o, i, AX)
            pr.op("V", em_rs, reads=[kx], writes=[L(0)])
            ts("V", lc[:, 1:2], lc[:, 0:1], r1024, None, ALU.mult,
               ALU.bypass, reads=[L(0)], writes=[L(1)])
            act(s_xnt["hi"][:, 0:D], xt, AF.Square, 0.0, 1.0, lc[:, 2:3],
                reads=[kx], writes=[("xnt", "hi"), L(2)])
            ts("V", lc[:, 3:4], lc[:, 2:3], r1024, None, ALU.mult,
               ALU.bypass, reads=[L(2)], writes=[L(3)])
            tt("V", lc[:, 4:5], lc[:, 1:2], lc[:, 1:2], ALU.mult,
               reads=[L(1)], writes=[L(4)])
            tt("V", lc[:, 5:6], lc[:, 3:4], lc[:, 4:5], ALU.subtract,
               reads=[L(3), L(4)], writes=[L(5)])
            ts("V", lc[:, 5:6], lc[:, 5:6], 1.0, 1e-5, ALU.mult, ALU.add,
               reads=[L(5)], writes=[L(5)])
            def em_sq(e, o=lc[:, 6:7], i=lc[:, 5:6]):
                return e.sqrt(o, i)
            pr.op("S", em_sq, reads=[L(5)], writes=[L(6)])
            def em_rc(e, o=lc[:, 7:8], i=lc[:, 6:7]):
                return e.reciprocal(o, i)
            pr.op("V", em_rc, reads=[L(6)], writes=[L(7)])
            for hf in range(2):
                xnf = s_P0[:, 0:512]
                kf = ("P0", 0)
                xh = xt[:, hf * 512:(hf + 1) * 512]
                hi = s_xnt["hi"][:, hf * 512:(hf + 1) * 512]
                lo = s_xnt["lo"][:, hf * 512:(hf + 1) * 512]
                ts("V", xnf, xh, lc[:, 1:2], None, ALU.subtract, ALU.bypass,
                   reads=[kx, L(1), kf], writes=[kf])
                ts("V", hi, xnf, lc[:, 7:8], None, ALU.mult,
                   ALU.bypass, reads=[kf, L(7)], writes=[("xnt", "hi")])
                stt("V", lo, xnf, lc[:, 7:8], hi, ALU.mult, ALU.subtract,
                    reads=[kf, L(7), ("xnt", "hi")], writes=[("xnt", "lo")])

        def phase_X_tr(gi, cc):
            pos = gi % 2
            for ip, p in enumerate(("hi", "lo")):
                bk = ("pb", ip)
                for ic in range(8):
                    src_ = s_xnt[p][:, ic * 128:(ic + 1) * 128]
                    dst = psb[ip][:, ic * 128:(ic + 1) * 128]
                    tr(bk, dst, src_, [("xnt", p)],
                       first=(ic == 0), close=(ic == 7), final=(ic == 7))
            cp("V", s_pre[:], psb[1][:, 0:1024],
               reads=[("pb", 1)], writes=[("pre", 0), ("pre", 1)])
            dsts = bass.AP(s_xn, pos * 256 + cc * 128,
                           [[8 * 512, 128], [512, 8], [1, 128]])
            src0 = bass.AP(psb[0], 0, [[1024, 128], [128, 8], [1, 128]])
            src1 = bass.AP(s_pre, 0, [[1024, 128], [128, 8], [1, 128]])
            tt("V", dsts, src0, src1, ALU.add,
               reads=[("pb", 0), ("pb", 1), ("pre", 0), ("pre", 1)],
               writes=[("xn", pos, cc)])

        def xblock(gi, cc):
            phase_X_ln(gi, cc)
            phase_X_tr(gi, cc)



        def phase_P_qk(gi):
            pos = gi % 2
            for nm, okey in (("wq", "q"), ("wk", "k")):
                bmap = (0, 1, 2, 3) if okey == "q" else (4, 5, 0, 1)
                for ic in range(8):
                    for hf in range(2):
                        b = panel_use(nm, ic, hf)
                        for mi in range(4):
                            m = hf * 4 + mi
                            bank = bmap[m // 2]
                            bk = ("ps", bank)
                            lh = s_wbuf[b][:, mi * 128:(mi + 1) * 128]
                            rh = s_xn[:, ic * 512 + pos * 256:
                                      ic * 512 + pos * 256 + 256]
                            out = ps[bank][:, (m % 2) * 256:(m % 2 + 1) * 256]
                            rds = ([("wbuf", b)] + xnk(pos)) if mi % 2 == 0 else []
                            mm(bk, out, lh, rh, rds,
                               start=(ic == 0 and m % 2 == 0),
                               stop=(ic == 7 and m % 2 == 1),
                               close=(m % 2 == 1),
                               final=(ic == 7 and m % 2 == 1))
                        pf_issue(1)
                for pr4 in range(4):
                    bank = bmap[pr4]
                    bk = ("ps", bank)
                    if okey == "q":
                        dst = s_q[:, pr4 * 512:(pr4 + 1) * 512]
                        src = ps[bank][:, 0:512]
                    else:
                        dst = bass.AP(s_k, 2 * pr4 * 384 + 128,
                                      [[8 * 384, 128], [384, 2], [1, 256]])
                        src = bass.AP(ps[bank], 0,
                                      [[512, 128], [256, 2], [1, 256]])
                    cp("S", dst, src, reads=[bk], writes=[(okey, pr4)])

        def phase_P_v(gi):
            pos = gi % 2
            vch0 = 2 * gi
            for ic in range(8):
                for hf in range(2):
                    b = panel_use("wv", ic, hf)
                    for tc in range(2):
                        bank = tc * 2 + hf
                        lh = s_xn[:, ic * 512 + pos * 256 + tc * 128:
                                  ic * 512 + pos * 256 + (tc + 1) * 128]
                        out = ps[bank][:, 0:512]
                        mm(("ps", bank), out, lh, s_wbuf[b][:],
                           [("wbuf", b)] + xnk(pos),
                           start=(ic == 0), stop=(ic == 7), close=True,
                           final=(ic == 7))
                    pf_issue(1)
            for tc in (1, 0):
                sl = (vch0 + tc) % 3
                for hf in range(2):
                    bank = tc * 2 + hf
                    bk = ("ps", bank)
                    hi = s_v["hi"][:, sl * D + hf * 512:sl * D + (hf + 1) * 512]
                    lo = s_v["lo"][:, sl * D + hf * 512:sl * D + (hf + 1) * 512]
                    cp("S", hi, ps[bank][:, 0:512], reads=[bk],
                       writes=[("v", "hi", sl, hf)])
                    stt("V", lo, ps[bank][:, 0:512], 1.0, hi, ALU.mult,
                        ALU.subtract, reads=[bk, ("v", "hi", sl, hf)],
                        writes=[("v", "lo", sl, hf)])

        def phase_A(gi):
            sq, g = groups[gi]
            def body(cc):
                vch = 2 * gi + cc
                seq_start = (g == 0 and cc == 0)
                msk_s = s_msk0 if seq_start else s_msk
                mkey = ("msk0",) if seq_start else ("msk",)

                def vap(p, h, blk, vch=vch):
                    sl = (vch - 1 + blk) % 3
                    c0 = h * HD
                    return (s_v[p][:, sl * D + c0:sl * D + c0 + HD],
                            ("v", p, sl, c0 // 512))

                def s_wave(w, cc=cc, msk_s=msk_s, mkey=mkey):
                    sbank = 2 + (w % 4)
                    sbk = ("ps", sbank)
                    for hh in range(2):
                        h = 2 * w + hh
                        pb = 64 * (h % 2)
                        m = h // 2
                        tp = (pb, 0) if pb else None
                        qh = s_q[pb:pb + 64, m * 256 + cc * 128:
                                 m * 256 + (cc + 1) * 128]
                        kh = s_k[pb:pb + 64, m * 384 + cc * 128:
                                 m * 384 + cc * 128 + 256]
                        omsk = ps[sbank][:, hh * 256:hh * 256 + 256]
                        rds = [("q", m // 2), ("k", m // 2)]
                        if cc == 0:
                            rds.append(("kcar",))
                        mm(sbk, omsk, qh, kh, rds, start=(hh == 0),
                           stop=False, close=False, final=False, tp=tp)
                        mm(sbk, omsk, s_cidr[:], msk_s[:],
                           [mkey, ("cidr",)], start=False, stop=(hh == 1),
                           close=True, final=(hh == 1))

                def head_work(w, cc=cc, vap=vap):
                    sbank = 2 + (w % 4)
                    sbk = ("ps", sbank)
                    for hh in range(2):
                        h = 2 * w + hh
                        slot = h % 2
                        act(s_P0[:, slot * 256:(slot + 1) * 256],
                            ps[sbank][:, hh * 256:hh * 256 + 256],
                            AF.Exp, s_n12[:, 0:1], 0.125,
                            s_sums[:, h:h + 1],
                            reads=[sbk, ("n12",)],
                            writes=[("P0", slot), ("sums", h)])
                    def em_rcp(e, o=s_rcp[:, 2 * w:2 * w + 2],
                               i=s_sums[:, 2 * w:2 * w + 2]):
                        return e.reciprocal(o, i)
                    pr.op("V", em_rcp,
                          reads=[("sums", 2 * w), ("sums", 2 * w + 1)],
                          writes=[("rcp", w)])
                    for hh in range(2):
                        h = 2 * w + hh
                        slot = h % 2
                        p0 = s_P0[:, slot * 256:(slot + 1) * 256]
                        ph = s_Ph[:, slot * 256:(slot + 1) * 256]
                        pl = s_Pl[:, slot * 256:(slot + 1) * 256]
                        rc = s_rcp[:, h:h + 1]
                        ts("V", ph, p0, rc, None, ALU.mult, ALU.bypass,
                           reads=[("P0", slot), ("rcp", w)],
                           writes=[("Ph", slot)])
                        stt("V", pl, p0, rc, ph, ALU.mult, ALU.subtract,
                            reads=[("P0", slot), ("rcp", w), ("Ph", slot)],
                            writes=[("Pl", slot)])
                    for hh in range(2):
                        h = 2 * w + hh
                        slot = h % 2
                        ptk = ("pb", slot)
                        for idx in range(4):
                            ti, blk = divmod(idx, 2)
                            sap = (s_Ph if ti == 0 else s_Pl)[
                                :, slot * 256 + blk * 128:
                                slot * 256 + (blk + 1) * 128]
                            dst = psb[slot][:, idx * 128:(idx + 1) * 128]
                            tr(ptk, dst, sap,
                               [("Ph", slot), ("Pl", slot)],
                               first=(idx == 0), close=(idx == 3),
                               final=(idx == 3))
                    for hh in range(2):
                        h = 2 * w + hh
                        slot = h % 2
                        ptk = ("pb", slot)
                        dpt = s_PT[:, slot * 512:(slot + 1) * 512]
                        cp("V", dpt, psb[slot][:, 0:512], reads=[ptk],
                           writes=[("PT", slot)])
                    for hh in range(2):
                        h = 2 * w + hh
                        slot = h % 2
                        obank = w // 4
                        obk = ("ps", obank)
                        pb = 64 * (h % 2)
                        qq = w % 4
                        oout = ps[obank][pb:pb + 64, qq * 128:(qq + 1) * 128]
                        otp = (0, pb) if pb else None
                        seqm = []
                        for blk in range(2):
                            vh, vkh = vap("hi", h, blk)
                            vl, vkl = vap("lo", h, blk)
                            pthi = s_PT[:, slot * 512 + blk * 128:
                                        slot * 512 + (blk + 1) * 128]
                            ptlo = s_PT[:, slot * 512 + 256 + blk * 128:
                                        slot * 512 + 256 + (blk + 1) * 128]
                            seqm.append((vh, pthi, [vkh, ("PT", slot)]))
                            seqm.append((vh, ptlo, []))
                            seqm.append((vl, pthi, [vkl]))
                        for si, (lh, rr, rds) in enumerate(seqm):
                            mm(obk, oout, lh, rr, rds,
                               start=(si == 0), stop=(si == 5),
                               close=(si == 5),
                               final=(w % 4 == 3 and hh == 1 and si == 5),
                               tp=otp)

                def oevac(cc=cc):
                    for ob in range(2):
                        obk = ("ps", ob)
                        dsts = bass.AP(s_O, ob * 4 * GRP + cc * 128,
                                       [[8 * GRP, 128], [GRP, 4], [1, 128]])
                        srcs = bass.AP(ps[ob], 0,
                                       [[512, 128], [128, 4], [1, 128]])
                        cp("S", dsts, srcs, reads=[obk],
                           writes=[("O", ob, cc)])
                return s_wave, head_work, oevac

            sw0, hw0, oe0 = body(0)
            sw1, hw1, oe1 = body(1)
            waves = [(sw0, hw0)] * 8 + [(sw1, hw1)] * 8
            for W in range(3):
                waves[W][0](W % 8)
            for W in range(16):
                if W + 3 < 16:
                    waves[W + 3][0]((W + 3) % 8)
                waves[W][1](W % 8)
                if W == 7:
                    oe0()
            oe1()
            dstc = bass.AP(s_k, 0, [[8 * 384, 128], [384, 8], [1, 128]])
            srcc = bass.AP(s_k, 256, [[8 * 384, 128], [384, 8], [1, 128]])
            pr.dma("Q1", dstc, srcc,
                   reads=[("k", b) for b in range(4)] + [("kcar",)],
                   writes=[("kcar",)])

        def phase_Wo(gi):
            gi2 = 2 * gi
            for ic in range(8):
                for hf in range(2):
                    b = panel_use("wo", ic, hf)
                    for tc in range(2):
                        bank = tc * 2 + hf
                        lh = s_O[:, ic * 256 + tc * 128:ic * 256 + (tc + 1) * 128]
                        mm(("ps", bank), ps[bank][:, 0:512], lh, s_wbuf[b][:],
                           [("wbuf", b), ("O", ic // 4, tc)],
                           start=(ic == 0), stop=(ic == 7), close=True,
                           final=(ic == 7))
                    pf_issue(1)
            for tc in range(2):
                slot = (gi2 + tc) % 6
                kx = ("xc", slot)
                for hf in range(2):
                    bank = tc * 2 + hf
                    bk = ("ps", bank)
                    xap = s_xc[:, slot * D + hf * 512:slot * D + (hf + 1) * 512]
                    tt("V", xap, ps[bank][:, 0:512], xap, ALU.add,
                       reads=[bk, kx], writes=[kx])

        def phase_F0(pj):
            def strip_dma(m):
                if m >= 32:
                    return
                b = m % 3
                assert ("d0sd", m, 1) in pr.bufs
                pr.dma("Q0" if m % 2 == 0 else "Q1", s_d0b[b][:],
                       d0s_d[m * 128:(m + 1) * 128, :],
                       reads=[("d0b", b), ("d0sd", m, 0), ("d0sd", m, 1)],
                       writes=[("d0b", b)])
            drain(6)
            for idx in range(3):
                strip_dma(idx)
            for m in range(32):
                drain(2)
                pf_issue(1)
                bank = 4 + (m % 2)
                bk = ("ps", bank)
                out = ps[bank][:, 0:512]
                b = m % 3
                for ic in range(8):
                    lh = s_d0b[b][:, ic * 128:(ic + 1) * 128]
                    rh = s_xn[:, ic * 512:(ic + 1) * 512]
                    rds = [("d0b", b)] + (xn_all if ic == 0 else [])
                    mm(bk, out, lh, rh, rds, start=(ic == 0), stop=(ic == 7),
                       close=(ic == 7), final=(ic == 7))
                strip_dma(m + 3)
                act(s_a0[:, m * 512:(m + 1) * 512], out, AF.Relu, 0.0,
                    s_S0[:, 0:1], None, reads=[bk, ("S0",)],
                    writes=[("a0", m)])

        def g1_mms(m2):
            def strip_dma(idx):
                if idx >= 128:
                    return
                m2_, q_ = divmod(idx, 4)
                b = idx % 3
                assert ("w1pmd", m2_, 2 * q_ + 1) in pr.bufs
                pr.dma("Q0" if idx % 2 == 0 else "Q1", s_k1b[b][:],
                       w1pm_d[m2_ * 128:(m2_ + 1) * 128,
                              q_ * 1024:(q_ + 1) * 1024],
                       reads=[("k1b", b), ("w1pmd", m2_, 2 * q_),
                              ("w1pmd", m2_, 2 * q_ + 1)],
                       writes=[("k1b", b)])
            drain(8)
            pf_issue(1)
            if m2 == 0:
                for idx in range(3):
                    strip_dma(idx)
            sl = m2 % 2
            assert ("w2pmd", m2, 1) in pr.bufs
            pr.dma("Q0" if m2 % 2 == 0 else "Q1", s_w2b[sl][:],
                   w2pm_d[m2 * 128:(m2 + 1) * 128, :],
                   reads=[("w2b", sl), ("w2pmd", m2, 0), ("w2pmd", m2, 1)],
                   writes=[("w2b", sl)])
            bank = m2 % 2
            bk = ("ps", bank)
            out = ps[bank][:, 0:512]
            for kc in range(32):
                q = kc // 8
                b = (m2 * 4 + q) % 3
                lh = s_k1b[b][:, (kc % 8) * 128:(kc % 8 + 1) * 128]
                rh = s_a0[:, kc * 512:(kc + 1) * 512]
                rds = [("k1b", b)] + (a0keys if kc == 0 else [])
                mm(bk, out, lh, rh, rds, start=(kc == 0), stop=(kc == 31),
                   close=(kc % 8 == 7), final=(kc == 31))
                if kc % 8 == 7:
                    strip_dma(m2 * 4 + q + 3)
            act(s_a1[:, sl * 512:(sl + 1) * 512], out, AF.Relu, 0.0,
                s_S1[:, 0:1], None, reads=[bk, ("S1",)], writes=[("a1", sl)])

        def g2_batch(pj, bt):
            batch = range(2 * bt, 2 * bt + 2)
            akeys = [("a1", m2 % 2) for m2 in batch]
            wkeys = [("w2b", m2 % 2) for m2 in batch]
            for ti in range(8):
                tc, oh = divmod(ti, 2)
                bank = 2 + (ti % 4)
                gk = ("ps", bank)
                for j, m2 in enumerate(batch):
                    sl = m2 % 2
                    mm(gk, ps[bank][:, 0:512],
                       s_a1[:, sl * 512 + tc * 128:sl * 512 + (tc + 1) * 128],
                       s_w2b[sl][:, oh * 512:(oh + 1) * 512],
                       (akeys + wkeys) if j == 0 else [],
                       start=(j == 0), stop=(j == 1), close=(j == 1),
                       final=True)
                slot = (4 * pj + tc) % 6
                xap = s_xc[:, slot * D + oh * 512:slot * D + (oh + 1) * 512]
                stt("V", xap, ps[bank][:, 0:512], s_S2[:, 0:1], xap,
                    ALU.mult, ALU.add, reads=[gk, ("S2",), ("xc", slot)],
                    writes=[("xc", slot)])

        def phase_F12(pj):
            gi0 = 2 * pj
            nxtA = gi0 + 2 if gi0 + 2 < len(groups) else None

            drain(40)

            def ystore(gi, cc):
                sq, g = groups[gi]
                slot = (2 * gi + cc) % 6
                r0 = sq * SEQ + g * GRP + cc * 128
                pr.dma("Q1", y_d[r0:r0 + 128, :],
                       s_xc[:, slot * D:(slot + 1) * D],
                       reads=[("xc", slot)], writes=[("ydone", gi, cc)])

            for bt in range(16):
                for m2 in range(2 * bt, 2 * bt + 2):
                    g1_mms(m2)
                g2_batch(pj, bt)
                if nxtA is not None:
                    if bt == 2:
                        phase_X_ln(nxtA, 0)
                    elif bt == 4:
                        phase_X_tr(nxtA, 0)
                    elif bt == 6:
                        phase_X_ln(nxtA, 1)
                    elif bt == 8:
                        phase_X_tr(nxtA, 1)
            for pos in range(2):
                for cc in range(2):
                    ystore(gi0 + pos, cc)

        pre_consts()
        build_pre_tasks()
        for cc in range(2):
            xblock(0, cc)
        pf_issue(4)
        for pj in range(npairs):
            gia, gib = 2 * pj, 2 * pj + 1
            phase_P_qk(gia)
            drain(18)
            phase_P_v(gia)
            drain(18)
            phase_A(gia)
            drain(18)
            phase_X_ln(gib, 0)
            phase_X_tr(gib, 0)
            phase_X_ln(gib, 1)
            phase_X_tr(gib, 1)
            drain(18)
            phase_Wo(gia)
            drain(18)
            phase_P_qk(gib)
            drain(18)
            phase_P_v(gib)
            drain(18)
            phase_A(gib)
            drain(18)
            phase_Wo(gib)
            drain(18)
            phase_F0(pj)
            phase_F12(pj)
        assert not pre_tasks, len(pre_tasks)

        pr.finalize_queues()
        pr.render()
    return nc




_CACHE = {}


def _consts():
    i = np.arange(128)[:, None]
    j = np.arange(256)[None, :]
    band = (j >= i + 1) & (j <= i + 128)
    valid0 = band & (j >= 128)
    cmsk = np.where(band, 0.0, MASK_RAW).astype(np.float32)
    cmsk0 = np.where(valid0, 0.0, MASK_RAW).astype(np.float32)
    cid = np.eye(128, dtype=ml_dtypes.bfloat16)
    cidr = np.eye(128, dtype=np.float32)
    cn12 = np.full((128, 1), EXP_SHIFT, dtype=np.float32)
    return cmsk, cmsk0, cid, cidr, cn12


def make_inputs(L0_attn_x, L0_attn_Wq, L0_attn_Wk, L0_attn_Wv, L0_attn_Wo,
                L0_ffn_k0, L0_ffn_k1, L0_ffn_k2, L0_ffn_s0, L0_ffn_s1,
                L0_ffn_s2):
    x = np.ascontiguousarray(np.asarray(L0_attn_x, dtype=np.float32))
    wqt = np.ascontiguousarray(np.asarray(L0_attn_Wq, np.float32).T)
    wkt = np.ascontiguousarray(np.asarray(L0_attn_Wk, np.float32).T)
    wvt = np.ascontiguousarray(np.asarray(L0_attn_Wv, np.float32).T)
    wot = np.ascontiguousarray(np.asarray(L0_attn_Wo, np.float32).T)
    k0 = np.asarray(L0_ffn_k0).astype(np.uint8)
    k0s = np.ascontiguousarray(
        k0.reshape(16, 128, 32, 128).transpose(2, 1, 0, 3)).reshape(HID, 2 * D)
    k1 = np.asarray(L0_ffn_k1).astype(np.uint8)
    k1s = np.ascontiguousarray(
        k1.reshape(32, 128, 32, 128).transpose(2, 1, 0, 3)).reshape(HID, HID)
    k2 = np.ascontiguousarray(np.asarray(L0_ffn_k2).astype(np.uint8))
    scal = np.zeros((128, 4), np.float32)
    scal[:, 0] = np.float32(L0_ffn_s0)
    scal[:, 1] = np.float32(L0_ffn_s1)
    scal[:, 2] = np.float32(L0_ffn_s2)
    cmsk, cmsk0, cid, cidr, cn12 = _consts()
    czero = np.zeros((128, 8 * 384), np.float32)
    shared = dict(wqt=wqt, wkt=wkt, wvt=wvt, wot=wot, k0s=k0s, k1s=k1s, k2=k2,
                  scal=scal, cmsk=cmsk, cmsk0=cmsk0, cid=cid, cidr=cidr,
                  cn12=cn12, czero=czero)
    in_maps = []
    for c in range(8):
        m = dict(shared)
        m["x"] = np.ascontiguousarray(x[2 * c:2 * c + 2].reshape(2 * SEQ, D))
        in_maps.append(m)
    return in_maps


def kernel(L0_attn_x, L0_attn_Wq, L0_attn_Wk, L0_attn_Wv, L0_attn_Wo,
           L0_ffn_k0, L0_ffn_k1, L0_ffn_k2, L0_ffn_s0, L0_ffn_s1, L0_ffn_s2):
    in_maps = make_inputs(L0_attn_x, L0_attn_Wq, L0_attn_Wk, L0_attn_Wv,
                          L0_attn_Wo, L0_ffn_k0, L0_ffn_k1, L0_ffn_k2,
                          L0_ffn_s0, L0_ffn_s1, L0_ffn_s2)
    if "nc" not in _CACHE:
        _CACHE["nc"] = build_program()
    nc = _CACHE["nc"]
    res = run_bass_kernel_spmd(nc, in_maps, core_ids=list(range(8)))
    out = np.empty((16, SEQ, D), np.float32)
    for c in range(8):
        out[2 * c:2 * c + 2] = np.asarray(res.results[c]["y"]).reshape(2, SEQ, D)
    return out


# version 27
# speedup vs baseline: 1.0303x; 1.0303x over previous
import numpy as np
import ml_dtypes
import concourse.bass as bass
import concourse.mybir as mybir
from concourse.bass_utils import run_bass_kernel_spmd

F32 = mybir.dt.float32
F32R = mybir.dt.float32r
BF16 = mybir.dt.bfloat16
U8 = mybir.dt.uint8
AF = mybir.ActivationFunctionType
ALU = mybir.AluOpType
AX = mybir.AxisListType.X

D = 1024
NH = 16
HD = 64
CH = 128
GRP = 256
HID = 4096
SEQ = 2048
NSEQ = 2
NGRP = SEQ // GRP
MASK_RAW = -262144.0
EXP_SHIFT = -12.0

ENG = ("T", "V", "S", "Q0", "Q1")
NPOOL = {"Q0": 12, "Q1": 8, "Q2": 8}
DIMS = (["T", "V", "S", "G"]
        + [(q, k) for q in ("Q0", "Q1", "Q2") for k in range(NPOOL[q])])


class Prog:

    def __init__(self, nc):
        self.nc = nc
        self.thunks = {e: [] for e in ENG}
        self.cnt = {e: 0 for e in ("T", "V", "S", "G")}
        self.dcnt = {q: 0 for q in ("Q0", "Q1", "Q2")}
        self.vc = {e: {d: 0 for d in DIMS} for e in ENG}
        self.snap = {d: [None] for d in DIMS}
        self.bufs = {}
        self.sems = {}
        self.pool_uses = {q: [0] * NPOOL[q] for q in ("Q0", "Q1", "Q2")}
        self.pend = {}

    def make_sems(self, estack):
        for e in ("T", "V", "S", "G"):
            self.sems[e] = estack.enter_context(self.nc.semaphore("sem" + e))
        for q in ("Q0", "Q1", "Q2"):
            for k in range(NPOOL[q]):
                self.sems[(q, k)] = estack.enter_context(
                    self.nc.semaphore("sem%s_%d" % (q, k)))

    def _need_for(self, reads, writes):
        need = {}
        for key in reads:
            b = self.bufs.get(key)
            if b and b[0] is not None:
                e, i = b[0]
                if i > need.get(e, 0):
                    need[e] = i
        for key in writes:
            b = self.bufs.get(key)
            if b:
                if b[0] is not None:
                    e, i = b[0]
                    if i > need.get(e, 0):
                        need[e] = i
                for e, i in b[1].items():
                    if i > need.get(e, 0):
                        need[e] = i
        return need

    def _emit_waits(self, eng, need):
        myvc = self.vc[eng]
        for d in sorted(need, key=str):
            i = need[d]
            if myvc[d] >= i:
                continue
            if isinstance(d, tuple):
                self.thunks[eng].append(("w", self.sems[d], 16 * i))
            else:
                self.thunks[eng].append(("w", self.sems[d], i))
            sn = self.snap[d][i]
            for dd, v in sn.items():
                if v > myvc[dd]:
                    myvc[dd] = v
            if i > myvc[d]:
                myvc[d] = i

    def _marker(self, eng):
        self.cnt[eng] += 1
        i = self.cnt[eng]
        sn = dict(self.vc[eng])
        sn[eng] = i
        self.snap[eng].append(sn)
        return (eng, i)

    def _register(self, reads, writes, marker):
        for key in writes:
            self.bufs[key] = [marker, {}]
        e, i = marker
        for key in reads:
            b = self.bufs.get(key)
            if b is None:
                b = [None, {}]
                self.bufs[key] = b
            if i > b[1].get(e, 0):
                b[1][e] = i

    def op(self, eng, emit, reads=(), writes=()):
        stream = "Q1" if eng == "G" else eng
        self._emit_waits(stream, self._need_for(reads, writes))
        self.cnt[eng] += 1
        i = self.cnt[eng]
        sn = dict(self.vc[stream])
        sn[eng] = i
        self.snap[eng].append(sn)
        self.thunks[stream].append(("i", emit, self.sems[eng], 1))
        self._register(reads, writes, (eng, i))

    def gop(self, eng, emit, reads=(), gkey=None, first=False, close=False,
            final=False):
        writes = (gkey,) if first else ()
        self._emit_waits(eng, self._need_for(reads, writes))
        pend = self.pend.setdefault((eng, gkey), set())
        pend.update(reads)
        if close:
            m = self._marker(eng)
            self.thunks[eng].append(("i", emit, self.sems[eng], 1))
            self._register(tuple(pend), (gkey,) if final else (), m)
            pend.clear()
            if final:
                self.pend.pop((eng, gkey), None)
        else:
            self.thunks[eng].append(("x", emit))

    def dma(self, q, out_ap, in_ap, reads=(), writes=()):
        stream = "S" if q == "Q2" else q
        R = NPOOL[q]
        idx = self.dcnt[q]
        self.dcnt[q] = idx + 1
        k = idx % R
        d = (q, k)
        prev_uses = self.pool_uses[q][k]
        if prev_uses > 0 and self.vc[stream][d] < prev_uses:
            self.thunks[stream].append(("w", self.sems[d], 16 * prev_uses))
            sn = self.snap[d][prev_uses]
            myvc = self.vc[stream]
            for dd, v in sn.items():
                if v > myvc[dd]:
                    myvc[dd] = v
        self._emit_waits(stream, self._need_for(reads, writes))
        self.thunks[stream].append(("d", out_ap, in_ap, self.sems[d]))
        u = prev_uses + 1
        self.pool_uses[q][k] = u
        sn = dict(self.vc[stream])
        sn[d] = u
        self.snap[d].append(sn)
        self._register(reads, writes, (d, u))

    def finalize_queues(self):
        for q in ("Q0", "Q1", "Q2"):
            stream = "S" if q == "Q2" else q
            for k in range(NPOOL[q]):
                u = self.pool_uses[q][k]
                if u > 0:
                    self.thunks[stream].append(("w", self.sems[(q, k)], 16 * u))

    def render(self):
        nc = self.nc

        def runner(e):
            lst = self.thunks[e]
            def run(engine_obj):
                for t in lst:
                    tag = t[0]
                    if tag == "w":
                        engine_obj.wait_ge(t[1], t[2])
                    elif tag == "i":
                        t[1](engine_obj).then_inc(t[2], t[3])
                    elif tag == "x":
                        t[1](engine_obj)
                    else:
                        engine_obj.dma_start(out=t[1], in_=t[2]).then_inc(t[3], 16)
            return run

        with nc.Block() as block:
            block.tensor(runner("T"))
            block.vector(runner("V"))
            block.scalar(runner("S"))
            block.sync(runner("Q0"))
            block.gpsimd(runner("Q1"))



def build_program(n_seq=NSEQ, n_grp=NGRP):
    import contextlib
    nc = bass.Bass("TRN2", target_bir_lowering=False, debug=False)

    x_d = nc.dram_tensor("x", [NSEQ * SEQ, D], F32, kind="ExternalInput").ap()
    y_d = nc.dram_tensor("y", [NSEQ * SEQ, D], F32, kind="ExternalOutput").ap()
    wt_d = {}
    for nm, t in (("wq", "wqt"), ("wk", "wkt"), ("wv", "wvt"), ("wo", "wot")):
        wt_d[nm] = nc.dram_tensor(t, [D, D], F32R, kind="ExternalInput").ap()
    k0s_d = nc.dram_tensor("k0s", [HID, 2 * D], U8, kind="ExternalInput").ap()
    k1s_d = nc.dram_tensor("k1s", [HID, HID], U8, kind="ExternalInput").ap()
    k2_d = nc.dram_tensor("k2", [HID, D], U8, kind="ExternalInput").ap()
    scal_d = nc.dram_tensor("scal", [128, 4], F32, kind="ExternalInput").ap()
    cmsk_d = nc.dram_tensor("cmsk", [128, 256], F32R, kind="ExternalInput").ap()
    cmsk0_d = nc.dram_tensor("cmsk0", [128, 256], F32R,
                             kind="ExternalInput").ap()
    cid_d = nc.dram_tensor("cid", [128, 128], BF16, kind="ExternalInput").ap()
    cidr_d = nc.dram_tensor("cidr", [128, 128], F32R, kind="ExternalInput").ap()
    cn12_d = nc.dram_tensor("cn12", [128, 1], F32, kind="ExternalInput").ap()
    czero_d = nc.dram_tensor("czero", [128, 8 * 384], F32R,
                             kind="ExternalInput").ap()
    d0s_d = nc.dram_tensor("d0s", [HID, D], F32R, kind="Internal").ap()
    w1pm_d = nc.dram_tensor("w1pm", [HID, HID], F32R, kind="Internal").ap()
    w2pm_d = nc.dram_tensor("w2pm", [HID, D], F32R, kind="Internal").ap()

    estack = contextlib.ExitStack()
    with estack:
        A = nc.alloc_sbuf_tensor
        s_cid = A("s_cid", [128, 128], BF16)
        s_cidr = A("s_cidr", [128, 128], F32R)
        s_msk = A("s_msk", [128, 256], F32R)
        s_msk0 = A("s_msk0", [128, 256], F32R)
        s_scal = A("s_scal", [128, 4], F32)
        s_n12 = A("s_n12", [128, 1], F32)
        s_S0 = A("s_S0", [128, 1], F32)
        s_S1 = A("s_S1", [128, 1], F32)
        s_S2 = A("s_S2", [128, 1], F32)
        s_ln = A("s_ln", [128, 16], F32)
        s_sums = A("s_sums", [128, NH], F32)
        s_rcp = A("s_rcp", [128, NH], F32)
        s_pre = A("s_pre", [128, 1024], BF16)
        s_prer = A("s_prer", [128, 1024], F32R)
        s_xc = A("s_xc", [128, 6 * D], F32)
        s_xnt = {p: A("s_xnt_" + p, [128, D], BF16) for p in ("hi", "lo")}
        s_xn = A("s_xn", [128, 8 * 512], F32R)
        s_q = A("s_q", [128, 8 * GRP], F32R)
        s_k = A("s_k", [128, 8 * 384], F32R)
        s_v = {p: A("s_v_" + p, [128, 3 * D], BF16) for p in ("hi", "lo")}
        s_O = A("s_O", [128, 8 * GRP], F32R)
        s_P0 = A("s_P0", [128, 2 * 256], F32)
        s_Ph = A("s_Ph", [128, 2 * 256], BF16)
        s_Pl = A("s_Pl", [128, 2 * 256], BF16)
        s_PT = A("s_PT", [128, 2 * 512], BF16)
        s_wbuf = [A("s_wbuf%d" % b, [128, 512], F32R) for b in range(4)]
        s_d0b = [A("s_d0b%d" % b, [128, 512], F32R) for b in range(6)]
        s_k1b = [A("s_k1b%d" % b, [128, 512], F32R) for b in range(6)]
        s_w2b = [A("s_w2b%d" % b, [128, D], F32R) for b in range(2)]
        s_a0 = A("s_a0", [128, 32 * 512], F32R)
        s_a1 = A("s_a1", [128, 2 * 512], F32R)

        ps = [nc.alloc_psum_tensor("ps%d" % i, [128, 512], F32) for i in range(6)]
        psb = [nc.alloc_psum_tensor("pb%d" % i, [128, 1024], BF16) for i in range(2)]

        pr = Prog(nc)
        pr.make_sems(estack)

        def ts(eng, out_ap, in_ap, s1, s2, op0, op1, reads, writes):
            def emit(e, o=out_ap, i=in_ap, s1=s1, s2=s2, op0=op0, op1=op1):
                return e.tensor_scalar(out=o, in0=i, scalar1=s1, scalar2=s2,
                                       op0=op0, op1=op1)
            pr.op(eng, emit, reads=reads, writes=writes)

        def stt(eng, out_ap, in0_ap, sc, in1_ap, op0, op1, reads, writes):
            def emit(e, o=out_ap, i0=in0_ap, sc=sc, i1=in1_ap, op0=op0, op1=op1):
                return e.scalar_tensor_tensor(out=o, in0=i0, scalar=sc, in1=i1,
                                              op0=op0, op1=op1)
            pr.op(eng, emit, reads=reads, writes=writes)

        def tt(eng, out_ap, a_ap, b_ap, alu, reads, writes):
            def emit(e, o=out_ap, a=a_ap, b=b_ap, alu=alu):
                return e.tensor_tensor(out=o, in0=a, in1=b, op=alu)
            pr.op(eng, emit, reads=reads, writes=writes)

        def cp(eng, out_ap, in_ap, reads, writes):
            if eng == "S":
                def emit(e, o=out_ap, i=in_ap):
                    return e.copy(out=o, in_=i)
            else:
                def emit(e, o=out_ap, i=in_ap):
                    return e.tensor_copy(out=o, in_=i)
            pr.op(eng, emit, reads=reads, writes=writes)

        def memset(eng, ap, val, reads, writes):
            def emit(e, a=ap, v=val):
                return e.memset(a, v)
            pr.op(eng, emit, reads=reads, writes=writes)

        def act(out_ap, in_ap, func, bias, scale, accum, reads, writes):
            def emit(e, o=out_ap, i=in_ap, f=func, b=bias, s=scale, a=accum):
                return e.activation(out=o, in_=i, func=f, bias=b, scale=s,
                                    accum_out=a)
            pr.op("S", emit, reads=reads, writes=writes)

        def mm(gkey, out_ap, lhsT, rhs, reads, start, stop, close, final, tp=None):
            def emit(e, o=out_ap, l=lhsT, r=rhs, st=start, sp=stop, tp=tp):
                return e.matmul(o, l, r, start=st, stop=sp, tile_position=tp)
            pr.gop("T", emit, reads=reads, gkey=gkey, first=start, close=close,
                   final=final)

        def tr(gkey, out_ap, in_ap, reads, first, close, final):
            def emit(e, o=out_ap, i=in_ap):
                return e.transpose(o, i, s_cid[:])
            pr.gop("T", emit, reads=tuple(reads) + (("cid",),), gkey=gkey,
                   first=first, close=close, final=final)

        groups = [(sq, g) for sq in range(n_seq) for g in range(n_grp)]
        assert len(groups) % 2 == 0
        npairs = len(groups) // 2

        def xnk(pos):
            return [("xn", pos, 0), ("xn", pos, 1)]
        xn_all = xnk(0) + xnk(1)
        a0keys = [("a0", m) for m in range(32)]
        r1024 = 1.0 / 1024.0

        pf_list = []
        for _pj in range(npairs):
            for _g in range(2):
                for _nm in ("wq", "wk", "wv", "wo"):
                    for _ic in range(8):
                        for _hf in range(2):
                            pf_list.append((_nm, _ic, _hf))
        pf_state = [0, 0]

        def pf_issue(n):
            for _ in range(n):
                i = pf_state[0]
                if i >= len(pf_list) or i - pf_state[1] >= 4:
                    return
                nm, ic, hf = pf_list[i]
                b = i % 4
                pr.dma("Q0", s_wbuf[b][:],
                       wt_d[nm][ic * 128:(ic + 1) * 128,
                                hf * 512:(hf + 1) * 512],
                       reads=[("wbuf", b)], writes=[("wbuf", b)])
                pf_state[0] = i + 1

        def panel_use(nm, ic, hf):
            i = pf_state[1]
            assert pf_list[i] == (nm, ic, hf), (pf_list[i], nm, ic, hf)
            assert i < pf_state[0]
            pf_state[1] = i + 1
            return i % 4

        def pre_consts():
            pr.dma("Q0", s_cid[:], cid_d, writes=[("cid",)])
            pr.dma("Q0", s_cidr[:], cidr_d, writes=[("cidr",)])
            pr.dma("Q0", s_msk[:], cmsk_d, writes=[("msk",)])
            pr.dma("Q0", s_msk0[:], cmsk0_d, writes=[("msk0",)])
            pr.dma("Q0", s_scal[:], scal_d, writes=[("scal",)])
            pr.dma("Q0", s_n12[:], cn12_d, writes=[("n12",)])
            ts("V", s_S0[:], s_scal[:, 0:1], 4.0, None, ALU.mult, ALU.bypass,
               reads=[("scal",)], writes=[("S0",)])
            ts("V", s_S1[:], s_scal[:, 1:2], 2.0, None, ALU.mult, ALU.bypass,
               reads=[("scal",)], writes=[("S1",)])
            ts("V", s_S2[:], s_scal[:, 2:3], 1.0, None, ALU.mult, ALU.bypass,
               reads=[("scal",)], writes=[("S2",)])
            pr.dma("Q0", s_k[:], czero_d, reads=[],
                   writes=[("kcar",)] + [("k", b) for b in range(4)])
            for p in ("hi", "lo"):
                memset("V", s_v[p][:, 2 * D:3 * D], 0.0, reads=[],
                       writes=[("v", p, 2, 0), ("v", p, 2, 1)])

        pre_tasks = []

        tsk_n = [0]

        def task_d0(m, qh):
            def run():
                kt = ("pre", 0)
                kb = ("pre", 1)
                kr = ("prer", tsk_n[0] % 2)
                bt = s_pre[:, 0:512]
                bb = s_pre[:, 512:1024]
                br = s_prer[:, (tsk_n[0] % 2) * 512:(tsk_n[0] % 2) * 512 + 512]
                r0 = m * 128
                pr.dma("Q1", bt, k0s_d[r0:r0 + 128, qh * 512:qh * 512 + 512],
                       reads=[kt], writes=[kt])
                pr.dma("Q1", bb, k0s_d[r0:r0 + 128,
                                       1024 + qh * 512:1024 + qh * 512 + 512],
                       reads=[kb], writes=[kb])
                tt("V", br, bt, bb, ALU.subtract, reads=[kt, kb, kr],
                   writes=[kr])
                pr.dma("Q0" if tsk_n[0] % 2 else "Q1",
                       d0s_d[r0:r0 + 128, qh * 512:(qh + 1) * 512], br,
                       reads=[kr], writes=[("d0sd", m, qh)])
                tsk_n[0] += 1
            return run

        def task_pm(src_d, dst_d, rc, c0, wkey):
            def run():
                h = tsk_n[0] % 2
                kk = ("pre", h)
                kr = ("prer", h)
                buf = s_pre[:, h * 512:(h + 1) * 512]
                bufr = s_prer[:, h * 512:(h + 1) * 512]
                pr.dma("Q1", buf, src_d[rc:rc + 128, c0:c0 + 512], reads=[kk],
                       writes=[kk])
                if h == 0:
                    act(bufr, buf, AF.Copy, -1.0, 2.0, None,
                        reads=[kk, kr], writes=[kr])
                else:
                    ts("V", bufr, buf, 2.0, -1.0, ALU.mult, ALU.add,
                       reads=[kk, kr], writes=[kr])
                pr.dma("Q0" if tsk_n[0] % 2 else "Q1",
                       dst_d[rc:rc + 128, c0:c0 + 512], bufr,
                       reads=[kr], writes=[wkey])
                tsk_n[0] += 1
            return run

        def build_pre_tasks():
            for m in range(32):
                for qh in range(2):
                    pre_tasks.append(task_d0(m, qh))
            for oc in range(32):
                for hf2 in range(2):
                    pre_tasks.append(task_pm(k2_d, w2pm_d, oc * 128,
                                             hf2 * 512, ("w2pmd", oc, hf2)))
            for m2 in range(32):
                for e in range(8):
                    pre_tasks.append(task_pm(k1s_d, w1pm_d, m2 * 128,
                                             e * 512, ("w1pmd", m2, e)))

        def drain(n):
            for _ in range(n):
                if pre_tasks:
                    pre_tasks.pop(0)()

        def phase_X_ln(gi, cc):
            sq, g = groups[gi]
            gch = 2 * gi + cc
            slot = gch % 6
            xt = s_xc[:, slot * D:(slot + 1) * D]
            kx = ("xc", slot)
            r0 = sq * SEQ + g * GRP + cc * 128
            L = lambda i, cc=cc: ("ln", cc, i)
            lc = s_ln[:, cc * 8:(cc + 1) * 8]
            pr.dma("Q0", xt, x_d[r0:r0 + 128, :], reads=[kx], writes=[kx])
            def em_rs(e, o=lc[:, 0:1], i=xt):
                return e.reduce_sum(o, i, AX)
            pr.op("V", em_rs, reads=[kx], writes=[L(0)])
            ts("V", lc[:, 1:2], lc[:, 0:1], r1024, None, ALU.mult,
               ALU.bypass, reads=[L(0)], writes=[L(1)])
            act(s_xnt["hi"][:, 0:D], xt, AF.Square, 0.0, 1.0, lc[:, 2:3],
                reads=[kx], writes=[("xnt", "hi"), L(2)])
            ts("V", lc[:, 3:4], lc[:, 2:3], r1024, None, ALU.mult,
               ALU.bypass, reads=[L(2)], writes=[L(3)])
            tt("V", lc[:, 4:5], lc[:, 1:2], lc[:, 1:2], ALU.mult,
               reads=[L(1)], writes=[L(4)])
            tt("V", lc[:, 5:6], lc[:, 3:4], lc[:, 4:5], ALU.subtract,
               reads=[L(3), L(4)], writes=[L(5)])
            ts("V", lc[:, 5:6], lc[:, 5:6], 1.0, 1e-5, ALU.mult, ALU.add,
               reads=[L(5)], writes=[L(5)])
            def em_sq(e, o=lc[:, 6:7], i=lc[:, 5:6]):
                return e.sqrt(o, i)
            pr.op("S", em_sq, reads=[L(5)], writes=[L(6)])
            def em_rc(e, o=lc[:, 7:8], i=lc[:, 6:7]):
                return e.reciprocal(o, i)
            pr.op("V", em_rc, reads=[L(6)], writes=[L(7)])
            for hf in range(2):
                xnf = s_P0[:, 0:512]
                kf = ("P0", 0)
                xh = xt[:, hf * 512:(hf + 1) * 512]
                hi = s_xnt["hi"][:, hf * 512:(hf + 1) * 512]
                lo = s_xnt["lo"][:, hf * 512:(hf + 1) * 512]
                ts("V", xnf, xh, lc[:, 1:2], None, ALU.subtract, ALU.bypass,
                   reads=[kx, L(1), kf], writes=[kf])
                ts("V", hi, xnf, lc[:, 7:8], None, ALU.mult,
                   ALU.bypass, reads=[kf, L(7)], writes=[("xnt", "hi")])
                stt("V", lo, xnf, lc[:, 7:8], hi, ALU.mult, ALU.subtract,
                    reads=[kf, L(7), ("xnt", "hi")], writes=[("xnt", "lo")])

        def phase_X_tr(gi, cc):
            pos = gi % 2
            for ip, p in enumerate(("hi", "lo")):
                bk = ("pb", ip)
                for ic in range(8):
                    src_ = s_xnt[p][:, ic * 128:(ic + 1) * 128]
                    dst = psb[ip][:, ic * 128:(ic + 1) * 128]
                    tr(bk, dst, src_, [("xnt", p)],
                       first=(ic == 0), close=(ic == 7), final=(ic == 7))
            cp("V", s_pre[:], psb[1][:, 0:1024],
               reads=[("pb", 1)], writes=[("pre", 0), ("pre", 1)])
            dsts = bass.AP(s_xn, pos * 256 + cc * 128,
                           [[8 * 512, 128], [512, 8], [1, 128]])
            src0 = bass.AP(psb[0], 0, [[1024, 128], [128, 8], [1, 128]])
            src1 = bass.AP(s_pre, 0, [[1024, 128], [128, 8], [1, 128]])
            tt("V", dsts, src0, src1, ALU.add,
               reads=[("pb", 0), ("pb", 1), ("pre", 0), ("pre", 1)],
               writes=[("xn", pos, cc)])

        def xblock(gi, cc):
            phase_X_ln(gi, cc)
            phase_X_tr(gi, cc)



        def phase_P_qk(gi):
            pos = gi % 2
            for nm, okey in (("wq", "q"), ("wk", "k")):
                bmap = (0, 1, 2, 3) if okey == "q" else (4, 5, 0, 1)
                for ic in range(8):
                    for hf in range(2):
                        b = panel_use(nm, ic, hf)
                        for mi in range(4):
                            m = hf * 4 + mi
                            bank = bmap[m // 2]
                            bk = ("ps", bank)
                            lh = s_wbuf[b][:, mi * 128:(mi + 1) * 128]
                            rh = s_xn[:, ic * 512 + pos * 256:
                                      ic * 512 + pos * 256 + 256]
                            out = ps[bank][:, (m % 2) * 256:(m % 2 + 1) * 256]
                            rds = ([("wbuf", b)] + xnk(pos)) if mi % 2 == 0 else []
                            mm(bk, out, lh, rh, rds,
                               start=(ic == 0 and m % 2 == 0),
                               stop=(ic == 7 and m % 2 == 1),
                               close=(m % 2 == 1),
                               final=(ic == 7 and m % 2 == 1))
                        pf_issue(1)
                for pr4 in range(4):
                    bank = bmap[pr4]
                    bk = ("ps", bank)
                    if okey == "q":
                        dst = s_q[:, pr4 * 512:(pr4 + 1) * 512]
                        src = ps[bank][:, 0:512]
                    else:
                        dst = bass.AP(s_k, 2 * pr4 * 384 + 128,
                                      [[8 * 384, 128], [384, 2], [1, 256]])
                        src = bass.AP(ps[bank], 0,
                                      [[512, 128], [256, 2], [1, 256]])
                    cp("S", dst, src, reads=[bk], writes=[(okey, pr4)])

        def phase_P_v(gi):
            pos = gi % 2
            vch0 = 2 * gi
            for ic in range(8):
                for hf in range(2):
                    b = panel_use("wv", ic, hf)
                    for tc in range(2):
                        bank = tc * 2 + hf
                        lh = s_xn[:, ic * 512 + pos * 256 + tc * 128:
                                  ic * 512 + pos * 256 + (tc + 1) * 128]
                        out = ps[bank][:, 0:512]
                        mm(("ps", bank), out, lh, s_wbuf[b][:],
                           [("wbuf", b)] + xnk(pos),
                           start=(ic == 0), stop=(ic == 7), close=True,
                           final=(ic == 7))
                    pf_issue(1)
            for tc in (1, 0):
                sl = (vch0 + tc) % 3
                for hf in range(2):
                    bank = tc * 2 + hf
                    bk = ("ps", bank)
                    hi = s_v["hi"][:, sl * D + hf * 512:sl * D + (hf + 1) * 512]
                    lo = s_v["lo"][:, sl * D + hf * 512:sl * D + (hf + 1) * 512]
                    cp("S", hi, ps[bank][:, 0:512], reads=[bk],
                       writes=[("v", "hi", sl, hf)])
                    stt("V", lo, ps[bank][:, 0:512], 1.0, hi, ALU.mult,
                        ALU.subtract, reads=[bk, ("v", "hi", sl, hf)],
                        writes=[("v", "lo", sl, hf)])

        def phase_A(gi):
            sq, g = groups[gi]
            def body(cc):
                vch = 2 * gi + cc
                seq_start = (g == 0 and cc == 0)
                msk_s = s_msk0 if seq_start else s_msk
                mkey = ("msk0",) if seq_start else ("msk",)

                def vap(p, h, blk, vch=vch):
                    sl = (vch - 1 + blk) % 3
                    c0 = h * HD
                    return (s_v[p][:, sl * D + c0:sl * D + c0 + HD],
                            ("v", p, sl, c0 // 512))

                def s_wave(w, cc=cc, msk_s=msk_s, mkey=mkey):
                    sbank = 2 + (w % 4)
                    sbk = ("ps", sbank)
                    for hh in range(2):
                        h = 2 * w + hh
                        pb = 64 * (h % 2)
                        m = h // 2
                        tp = (pb, 0) if pb else None
                        qh = s_q[pb:pb + 64, m * 256 + cc * 128:
                                 m * 256 + (cc + 1) * 128]
                        kh = s_k[pb:pb + 64, m * 384 + cc * 128:
                                 m * 384 + cc * 128 + 256]
                        omsk = ps[sbank][:, hh * 256:hh * 256 + 256]
                        rds = [("q", m // 2), ("k", m // 2)]
                        if cc == 0:
                            rds.append(("kcar",))
                        mm(sbk, omsk, qh, kh, rds, start=(hh == 0),
                           stop=False, close=False, final=False, tp=tp)
                        mm(sbk, omsk, s_cidr[:], msk_s[:],
                           [mkey, ("cidr",)], start=False, stop=(hh == 1),
                           close=True, final=(hh == 1))

                def head_work(w, cc=cc, vap=vap):
                    sbank = 2 + (w % 4)
                    sbk = ("ps", sbank)
                    for hh in range(2):
                        h = 2 * w + hh
                        slot = h % 2
                        act(s_P0[:, slot * 256:(slot + 1) * 256],
                            ps[sbank][:, hh * 256:hh * 256 + 256],
                            AF.Exp, s_n12[:, 0:1], 0.125,
                            s_sums[:, h:h + 1],
                            reads=[sbk, ("n12",)],
                            writes=[("P0", slot), ("sums", h)])
                    def em_rcp(e, o=s_rcp[:, 2 * w:2 * w + 2],
                               i=s_sums[:, 2 * w:2 * w + 2]):
                        return e.reciprocal(o, i)
                    pr.op("V", em_rcp,
                          reads=[("sums", 2 * w), ("sums", 2 * w + 1)],
                          writes=[("rcp", w)])
                    for hh in range(2):
                        h = 2 * w + hh
                        slot = h % 2
                        p0 = s_P0[:, slot * 256:(slot + 1) * 256]
                        ph = s_Ph[:, slot * 256:(slot + 1) * 256]
                        pl = s_Pl[:, slot * 256:(slot + 1) * 256]
                        rc = s_rcp[:, h:h + 1]
                        ts("V", ph, p0, rc, None, ALU.mult, ALU.bypass,
                           reads=[("P0", slot), ("rcp", w)],
                           writes=[("Ph", slot)])
                        stt("V", pl, p0, rc, ph, ALU.mult, ALU.subtract,
                            reads=[("P0", slot), ("rcp", w), ("Ph", slot)],
                            writes=[("Pl", slot)])
                    for hh in range(2):
                        h = 2 * w + hh
                        slot = h % 2
                        ptk = ("pb", slot)
                        for idx in range(4):
                            ti, blk = divmod(idx, 2)
                            sap = (s_Ph if ti == 0 else s_Pl)[
                                :, slot * 256 + blk * 128:
                                slot * 256 + (blk + 1) * 128]
                            dst = psb[slot][:, idx * 128:(idx + 1) * 128]
                            tr(ptk, dst, sap,
                               [("Ph", slot), ("Pl", slot)],
                               first=(idx == 0), close=(idx == 3),
                               final=(idx == 3))
                    for hh in range(2):
                        h = 2 * w + hh
                        slot = h % 2
                        ptk = ("pb", slot)
                        dpt = s_PT[:, slot * 512:(slot + 1) * 512]
                        cp("V", dpt, psb[slot][:, 0:512], reads=[ptk],
                           writes=[("PT", slot)])
                    for hh in range(2):
                        h = 2 * w + hh
                        slot = h % 2
                        obank = w // 4
                        obk = ("ps", obank)
                        pb = 64 * (h % 2)
                        qq = w % 4
                        oout = ps[obank][pb:pb + 64, qq * 128:(qq + 1) * 128]
                        otp = (0, pb) if pb else None
                        seqm = []
                        for blk in range(2):
                            vh, vkh = vap("hi", h, blk)
                            vl, vkl = vap("lo", h, blk)
                            pthi = s_PT[:, slot * 512 + blk * 128:
                                        slot * 512 + (blk + 1) * 128]
                            ptlo = s_PT[:, slot * 512 + 256 + blk * 128:
                                        slot * 512 + 256 + (blk + 1) * 128]
                            seqm.append((vh, pthi, [vkh, ("PT", slot)]))
                            seqm.append((vh, ptlo, []))
                            seqm.append((vl, pthi, [vkl]))
                        for si, (lh, rr, rds) in enumerate(seqm):
                            mm(obk, oout, lh, rr, rds,
                               start=(si == 0), stop=(si == 5),
                               close=(si == 5),
                               final=(w % 4 == 3 and hh == 1 and si == 5),
                               tp=otp)

                def oevac(cc=cc):
                    for ob in range(2):
                        obk = ("ps", ob)
                        dsts = bass.AP(s_O, ob * 4 * GRP + cc * 128,
                                       [[8 * GRP, 128], [GRP, 4], [1, 128]])
                        srcs = bass.AP(ps[ob], 0,
                                       [[512, 128], [128, 4], [1, 128]])
                        cp("S", dsts, srcs, reads=[obk],
                           writes=[("O", ob, cc)])
                return s_wave, head_work, oevac

            sw0, hw0, oe0 = body(0)
            sw1, hw1, oe1 = body(1)
            waves = [(sw0, hw0)] * 8 + [(sw1, hw1)] * 8
            for W in range(3):
                waves[W][0](W % 8)
            for W in range(16):
                if W + 3 < 16:
                    waves[W + 3][0]((W + 3) % 8)
                waves[W][1](W % 8)
                if W == 7:
                    oe0()
            oe1()
            dstc = bass.AP(s_k, 0, [[8 * 384, 128], [384, 8], [1, 128]])
            srcc = bass.AP(s_k, 256, [[8 * 384, 128], [384, 8], [1, 128]])
            pr.dma("Q1", dstc, srcc,
                   reads=[("k", b) for b in range(4)] + [("kcar",)],
                   writes=[("kcar",)])

        def phase_Wo(gi):
            gi2 = 2 * gi
            for ic in range(8):
                for hf in range(2):
                    b = panel_use("wo", ic, hf)
                    for tc in range(2):
                        bank = tc * 2 + hf
                        lh = s_O[:, ic * 256 + tc * 128:ic * 256 + (tc + 1) * 128]
                        mm(("ps", bank), ps[bank][:, 0:512], lh, s_wbuf[b][:],
                           [("wbuf", b), ("O", ic // 4, tc)],
                           start=(ic == 0), stop=(ic == 7), close=True,
                           final=(ic == 7))
                    pf_issue(1)
            for tc in range(2):
                slot = (gi2 + tc) % 6
                kx = ("xc", slot)
                for hf in range(2):
                    bank = tc * 2 + hf
                    bk = ("ps", bank)
                    xap = s_xc[:, slot * D + hf * 512:slot * D + (hf + 1) * 512]
                    tt("V", xap, ps[bank][:, 0:512], xap, ALU.add,
                       reads=[bk, kx], writes=[kx])

        def phase_F0(pj):
            def strip_dma(idx):
                if idx >= 64:
                    return
                m, h = divmod(idx, 2)
                b = idx % 6
                assert ("d0sd", m, h) in pr.bufs
                pr.dma("Q0" if idx % 2 == 0 else "Q1", s_d0b[b][:],
                       d0s_d[m * 128:(m + 1) * 128, h * 512:(h + 1) * 512],
                       reads=[("d0b", b), ("d0sd", m, h)],
                       writes=[("d0b", b)])
            drain(5)
            for idx in range(5):
                strip_dma(idx)
            for m in range(32):
                drain(2)
                pf_issue(1)
                bank = 4 + (m % 2)
                bk = ("ps", bank)
                out = ps[bank][:, 0:512]
                for ic in range(8):
                    h = ic // 4
                    b = (m * 2 + h) % 6
                    lh = s_d0b[b][:, (ic % 4) * 128:(ic % 4 + 1) * 128]
                    rh = s_xn[:, ic * 512:(ic + 1) * 512]
                    rds = [("d0b", b)] + (xn_all if ic == 0 else [])
                    mm(bk, out, lh, rh, rds, start=(ic == 0), stop=(ic == 7),
                       close=(ic % 4 == 3), final=(ic == 7))
                    if ic % 4 == 3:
                        strip_dma(m * 2 + h + 5)
                act(s_a0[:, m * 512:(m + 1) * 512], out, AF.Relu, 0.0,
                    s_S0[:, 0:1], None, reads=[bk, ("S0",)],
                    writes=[("a0", m)])

        def g1_mms(m2):
            def strip_dma(idx):
                if idx >= 256:
                    return
                m2_, e_ = divmod(idx, 8)
                b = idx % 6
                assert ("w1pmd", m2_, e_) in pr.bufs
                pr.dma("Q0" if idx % 2 == 0 else "Q1", s_k1b[b][:],
                       w1pm_d[m2_ * 128:(m2_ + 1) * 128,
                              e_ * 512:(e_ + 1) * 512],
                       reads=[("k1b", b), ("w1pmd", m2_, e_)],
                       writes=[("k1b", b)])
            drain(8)
            pf_issue(1)
            if m2 == 0:
                for idx in range(5):
                    strip_dma(idx)
            sl = m2 % 2
            assert ("w2pmd", m2, 1) in pr.bufs
            pr.dma("Q0" if m2 % 2 == 0 else "Q1", s_w2b[sl][:],
                   w2pm_d[m2 * 128:(m2 + 1) * 128, :],
                   reads=[("w2b", sl), ("w2pmd", m2, 0), ("w2pmd", m2, 1)],
                   writes=[("w2b", sl)])
            bank = m2 % 2
            bk = ("ps", bank)
            out = ps[bank][:, 0:512]
            for kc in range(32):
                e = kc // 4
                b = (m2 * 8 + e) % 6
                lh = s_k1b[b][:, (kc % 4) * 128:(kc % 4 + 1) * 128]
                rh = s_a0[:, kc * 512:(kc + 1) * 512]
                rds = [("k1b", b)] + (a0keys if kc == 0 else [])
                mm(bk, out, lh, rh, rds, start=(kc == 0), stop=(kc == 31),
                   close=(kc % 4 == 3), final=(kc == 31))
                if kc % 4 == 3:
                    strip_dma(m2 * 8 + e + 5)
            act(s_a1[:, sl * 512:(sl + 1) * 512], out, AF.Relu, 0.0,
                s_S1[:, 0:1], None, reads=[bk, ("S1",)], writes=[("a1", sl)])

        def g2_batch(pj, bt):
            batch = range(2 * bt, 2 * bt + 2)
            akeys = [("a1", m2 % 2) for m2 in batch]
            wkeys = [("w2b", m2 % 2) for m2 in batch]
            for ti in range(8):
                tc, oh = divmod(ti, 2)
                bank = 2 + (ti % 4)
                gk = ("ps", bank)
                for j, m2 in enumerate(batch):
                    sl = m2 % 2
                    mm(gk, ps[bank][:, 0:512],
                       s_a1[:, sl * 512 + tc * 128:sl * 512 + (tc + 1) * 128],
                       s_w2b[sl][:, oh * 512:(oh + 1) * 512],
                       (akeys + wkeys) if j == 0 else [],
                       start=(j == 0), stop=(j == 1), close=(j == 1),
                       final=True)
                slot = (4 * pj + tc) % 6
                xap = s_xc[:, slot * D + oh * 512:slot * D + (oh + 1) * 512]
                stt("V", xap, ps[bank][:, 0:512], s_S2[:, 0:1], xap,
                    ALU.mult, ALU.add, reads=[gk, ("S2",), ("xc", slot)],
                    writes=[("xc", slot)])

        def phase_F12(pj):
            gi0 = 2 * pj
            nxtA = gi0 + 2 if gi0 + 2 < len(groups) else None

            drain(40)

            def ystore(gi, cc):
                sq, g = groups[gi]
                slot = (2 * gi + cc) % 6
                r0 = sq * SEQ + g * GRP + cc * 128
                pr.dma("Q1", y_d[r0:r0 + 128, :],
                       s_xc[:, slot * D:(slot + 1) * D],
                       reads=[("xc", slot)], writes=[("ydone", gi, cc)])

            for bt in range(16):
                for m2 in range(2 * bt, 2 * bt + 2):
                    g1_mms(m2)
                g2_batch(pj, bt)
                if nxtA is not None:
                    if bt == 2:
                        phase_X_ln(nxtA, 0)
                    elif bt == 4:
                        phase_X_tr(nxtA, 0)
                    elif bt == 6:
                        phase_X_ln(nxtA, 1)
                    elif bt == 8:
                        phase_X_tr(nxtA, 1)
            for pos in range(2):
                for cc in range(2):
                    ystore(gi0 + pos, cc)

        pre_consts()
        build_pre_tasks()
        for cc in range(2):
            xblock(0, cc)
        pf_issue(4)
        for pj in range(npairs):
            gia, gib = 2 * pj, 2 * pj + 1
            phase_P_qk(gia)
            drain(18)
            phase_P_v(gia)
            drain(18)
            phase_A(gia)
            drain(18)
            phase_X_ln(gib, 0)
            phase_X_tr(gib, 0)
            phase_X_ln(gib, 1)
            phase_X_tr(gib, 1)
            drain(18)
            phase_Wo(gia)
            drain(18)
            phase_P_qk(gib)
            drain(18)
            phase_P_v(gib)
            drain(18)
            phase_A(gib)
            drain(18)
            phase_Wo(gib)
            drain(18)
            phase_F0(pj)
            phase_F12(pj)
        assert not pre_tasks, len(pre_tasks)

        pr.finalize_queues()
        pr.render()
    return nc




_CACHE = {}


def _consts():
    i = np.arange(128)[:, None]
    j = np.arange(256)[None, :]
    band = (j >= i + 1) & (j <= i + 128)
    valid0 = band & (j >= 128)
    cmsk = np.where(band, 0.0, MASK_RAW).astype(np.float32)
    cmsk0 = np.where(valid0, 0.0, MASK_RAW).astype(np.float32)
    cid = np.eye(128, dtype=ml_dtypes.bfloat16)
    cidr = np.eye(128, dtype=np.float32)
    cn12 = np.full((128, 1), EXP_SHIFT, dtype=np.float32)
    return cmsk, cmsk0, cid, cidr, cn12


def make_inputs(L0_attn_x, L0_attn_Wq, L0_attn_Wk, L0_attn_Wv, L0_attn_Wo,
                L0_ffn_k0, L0_ffn_k1, L0_ffn_k2, L0_ffn_s0, L0_ffn_s1,
                L0_ffn_s2):
    x = np.ascontiguousarray(np.asarray(L0_attn_x, dtype=np.float32))
    wqt = np.ascontiguousarray(np.asarray(L0_attn_Wq, np.float32).T)
    wkt = np.ascontiguousarray(np.asarray(L0_attn_Wk, np.float32).T)
    wvt = np.ascontiguousarray(np.asarray(L0_attn_Wv, np.float32).T)
    wot = np.ascontiguousarray(np.asarray(L0_attn_Wo, np.float32).T)
    k0 = np.asarray(L0_ffn_k0).astype(np.uint8)
    k0s = np.ascontiguousarray(
        k0.reshape(16, 128, 32, 128).transpose(2, 1, 0, 3)).reshape(HID, 2 * D)
    k1 = np.asarray(L0_ffn_k1).astype(np.uint8)
    k1s = np.ascontiguousarray(
        k1.reshape(32, 128, 32, 128).transpose(2, 1, 0, 3)).reshape(HID, HID)
    k2 = np.ascontiguousarray(np.asarray(L0_ffn_k2).astype(np.uint8))
    scal = np.zeros((128, 4), np.float32)
    scal[:, 0] = np.float32(L0_ffn_s0)
    scal[:, 1] = np.float32(L0_ffn_s1)
    scal[:, 2] = np.float32(L0_ffn_s2)
    cmsk, cmsk0, cid, cidr, cn12 = _consts()
    czero = np.zeros((128, 8 * 384), np.float32)
    shared = dict(wqt=wqt, wkt=wkt, wvt=wvt, wot=wot, k0s=k0s, k1s=k1s, k2=k2,
                  scal=scal, cmsk=cmsk, cmsk0=cmsk0, cid=cid, cidr=cidr,
                  cn12=cn12, czero=czero)
    in_maps = []
    for c in range(8):
        m = dict(shared)
        m["x"] = np.ascontiguousarray(x[2 * c:2 * c + 2].reshape(2 * SEQ, D))
        in_maps.append(m)
    return in_maps


def kernel(L0_attn_x, L0_attn_Wq, L0_attn_Wk, L0_attn_Wv, L0_attn_Wo,
           L0_ffn_k0, L0_ffn_k1, L0_ffn_k2, L0_ffn_s0, L0_ffn_s1, L0_ffn_s2):
    in_maps = make_inputs(L0_attn_x, L0_attn_Wq, L0_attn_Wk, L0_attn_Wv,
                          L0_attn_Wo, L0_ffn_k0, L0_ffn_k1, L0_ffn_k2,
                          L0_ffn_s0, L0_ffn_s1, L0_ffn_s2)
    if "nc" not in _CACHE:
        _CACHE["nc"] = build_program()
    nc = _CACHE["nc"]
    res = run_bass_kernel_spmd(nc, in_maps, core_ids=list(range(8)))
    out = np.empty((16, SEQ, D), np.float32)
    for c in range(8):
        out[2 * c:2 * c + 2] = np.asarray(res.results[c]["y"]).reshape(2, SEQ, D)
    return out


# version 28
# speedup vs baseline: 1.0639x; 1.0325x over previous
import numpy as np
import ml_dtypes
import concourse.bass as bass
import concourse.mybir as mybir
from concourse.bass_utils import run_bass_kernel_spmd

F32 = mybir.dt.float32
F32R = mybir.dt.float32r
BF16 = mybir.dt.bfloat16
U8 = mybir.dt.uint8
AF = mybir.ActivationFunctionType
ALU = mybir.AluOpType
AX = mybir.AxisListType.X

D = 1024
NH = 16
HD = 64
CH = 128
GRP = 256
HID = 4096
SEQ = 2048
NSEQ = 2
NGRP = SEQ // GRP
MASK_RAW = -262144.0
EXP_SHIFT = -12.0

ENG = ("T", "V", "S", "Q0", "Q1")
NPOOL = {"Q0": 12, "Q1": 8, "Q2": 8}
DIMS = (["T", "V", "S", "G"]
        + [(q, k) for q in ("Q0", "Q1", "Q2") for k in range(NPOOL[q])])


class Prog:

    def __init__(self, nc):
        self.nc = nc
        self.thunks = {e: [] for e in ENG}
        self.cnt = {e: 0 for e in ("T", "V", "S", "G")}
        self.dcnt = {q: 0 for q in ("Q0", "Q1", "Q2")}
        self.vc = {e: {d: 0 for d in DIMS} for e in ENG}
        self.snap = {d: [None] for d in DIMS}
        self.bufs = {}
        self.sems = {}
        self.pool_uses = {q: [0] * NPOOL[q] for q in ("Q0", "Q1", "Q2")}
        self.pend = {}

    def make_sems(self, estack):
        for e in ("T", "V", "S", "G"):
            self.sems[e] = estack.enter_context(self.nc.semaphore("sem" + e))
        for q in ("Q0", "Q1", "Q2"):
            for k in range(NPOOL[q]):
                self.sems[(q, k)] = estack.enter_context(
                    self.nc.semaphore("sem%s_%d" % (q, k)))

    def _need_for(self, reads, writes):
        need = {}
        for key in reads:
            b = self.bufs.get(key)
            if b and b[0] is not None:
                e, i = b[0]
                if i > need.get(e, 0):
                    need[e] = i
        for key in writes:
            b = self.bufs.get(key)
            if b:
                if b[0] is not None:
                    e, i = b[0]
                    if i > need.get(e, 0):
                        need[e] = i
                for e, i in b[1].items():
                    if i > need.get(e, 0):
                        need[e] = i
        return need

    def _emit_waits(self, eng, need):
        myvc = self.vc[eng]
        for d in sorted(need, key=str):
            i = need[d]
            if myvc[d] >= i:
                continue
            if isinstance(d, tuple):
                self.thunks[eng].append(("w", self.sems[d], 16 * i))
            else:
                self.thunks[eng].append(("w", self.sems[d], i))
            sn = self.snap[d][i]
            for dd, v in sn.items():
                if v > myvc[dd]:
                    myvc[dd] = v
            if i > myvc[d]:
                myvc[d] = i

    def _marker(self, eng):
        self.cnt[eng] += 1
        i = self.cnt[eng]
        sn = dict(self.vc[eng])
        sn[eng] = i
        self.snap[eng].append(sn)
        return (eng, i)

    def _register(self, reads, writes, marker):
        for key in writes:
            self.bufs[key] = [marker, {}]
        e, i = marker
        for key in reads:
            b = self.bufs.get(key)
            if b is None:
                b = [None, {}]
                self.bufs[key] = b
            if i > b[1].get(e, 0):
                b[1][e] = i

    def op(self, eng, emit, reads=(), writes=()):
        stream = "Q1" if eng == "G" else eng
        self._emit_waits(stream, self._need_for(reads, writes))
        self.cnt[eng] += 1
        i = self.cnt[eng]
        sn = dict(self.vc[stream])
        sn[eng] = i
        self.snap[eng].append(sn)
        self.thunks[stream].append(("i", emit, self.sems[eng], 1))
        self._register(reads, writes, (eng, i))

    def gop(self, eng, emit, reads=(), gkey=None, first=False, close=False,
            final=False):
        writes = (gkey,) if first else ()
        self._emit_waits(eng, self._need_for(reads, writes))
        pend = self.pend.setdefault((eng, gkey), set())
        pend.update(reads)
        if close:
            m = self._marker(eng)
            self.thunks[eng].append(("i", emit, self.sems[eng], 1))
            self._register(tuple(pend), (gkey,) if final else (), m)
            pend.clear()
            if final:
                self.pend.pop((eng, gkey), None)
        else:
            self.thunks[eng].append(("x", emit))

    def dma(self, q, out_ap, in_ap, reads=(), writes=()):
        stream = "S" if q == "Q2" else q
        R = NPOOL[q]
        idx = self.dcnt[q]
        self.dcnt[q] = idx + 1
        k = idx % R
        d = (q, k)
        prev_uses = self.pool_uses[q][k]
        if prev_uses > 0 and self.vc[stream][d] < prev_uses:
            self.thunks[stream].append(("w", self.sems[d], 16 * prev_uses))
            sn = self.snap[d][prev_uses]
            myvc = self.vc[stream]
            for dd, v in sn.items():
                if v > myvc[dd]:
                    myvc[dd] = v
        self._emit_waits(stream, self._need_for(reads, writes))
        self.thunks[stream].append(("d", out_ap, in_ap, self.sems[d]))
        u = prev_uses + 1
        self.pool_uses[q][k] = u
        sn = dict(self.vc[stream])
        sn[d] = u
        self.snap[d].append(sn)
        self._register(reads, writes, (d, u))

    def finalize_queues(self):
        for q in ("Q0", "Q1", "Q2"):
            stream = "S" if q == "Q2" else q
            for k in range(NPOOL[q]):
                u = self.pool_uses[q][k]
                if u > 0:
                    self.thunks[stream].append(("w", self.sems[(q, k)], 16 * u))

    def render(self):
        nc = self.nc

        def runner(e):
            lst = self.thunks[e]
            def run(engine_obj):
                for t in lst:
                    tag = t[0]
                    if tag == "w":
                        engine_obj.wait_ge(t[1], t[2])
                    elif tag == "i":
                        t[1](engine_obj).then_inc(t[2], t[3])
                    elif tag == "x":
                        t[1](engine_obj)
                    else:
                        engine_obj.dma_start(out=t[1], in_=t[2]).then_inc(t[3], 16)
            return run

        with nc.Block() as block:
            block.tensor(runner("T"))
            block.vector(runner("V"))
            block.scalar(runner("S"))
            block.sync(runner("Q0"))
            block.gpsimd(runner("Q1"))



def build_program(n_seq=NSEQ, n_grp=NGRP):
    import contextlib
    nc = bass.Bass("TRN2", target_bir_lowering=False, debug=False)

    x_d = nc.dram_tensor("x", [NSEQ * SEQ, D], F32, kind="ExternalInput").ap()
    y_d = nc.dram_tensor("y", [NSEQ * SEQ, D], F32, kind="ExternalOutput").ap()
    wt_d = {}
    for nm, t in (("wq", "wqt"), ("wk", "wkt"), ("wv", "wvt"), ("wo", "wot")):
        wt_d[nm] = nc.dram_tensor(t, [D, D], F32R, kind="ExternalInput").ap()
    k0s_d = nc.dram_tensor("k0s", [HID, 2 * D], U8, kind="ExternalInput").ap()
    k1s_d = nc.dram_tensor("k1s", [HID, HID], U8, kind="ExternalInput").ap()
    k2_d = nc.dram_tensor("k2", [HID, D], U8, kind="ExternalInput").ap()
    scal_d = nc.dram_tensor("scal", [128, 4], F32, kind="ExternalInput").ap()
    cmsk_d = nc.dram_tensor("cmsk", [128, 256], F32R, kind="ExternalInput").ap()
    cmsk0_d = nc.dram_tensor("cmsk0", [128, 256], F32R,
                             kind="ExternalInput").ap()
    cid_d = nc.dram_tensor("cid", [128, 128], BF16, kind="ExternalInput").ap()
    cidr_d = nc.dram_tensor("cidr", [128, 128], F32R, kind="ExternalInput").ap()
    cn12_d = nc.dram_tensor("cn12", [128, 1], F32, kind="ExternalInput").ap()
    czero_d = nc.dram_tensor("czero", [128, 8 * 384], F32R,
                             kind="ExternalInput").ap()
    d0s_d = nc.dram_tensor("d0s", [HID, D], F32R, kind="Internal").ap()
    w1pm_d = nc.dram_tensor("w1pm", [HID, HID], F32R, kind="Internal").ap()
    w2pm_d = nc.dram_tensor("w2pm", [HID, D], F32R, kind="Internal").ap()

    estack = contextlib.ExitStack()
    with estack:
        A = nc.alloc_sbuf_tensor
        s_cid = A("s_cid", [128, 128], BF16)
        s_cidr = A("s_cidr", [128, 128], F32R)
        s_msk = A("s_msk", [128, 256], F32R)
        s_msk0 = A("s_msk0", [128, 256], F32R)
        s_scal = A("s_scal", [128, 4], F32)
        s_n12 = A("s_n12", [128, 1], F32)
        s_S0 = A("s_S0", [128, 1], F32)
        s_S1 = A("s_S1", [128, 1], F32)
        s_S2 = A("s_S2", [128, 1], F32)
        s_ln = A("s_ln", [128, 16], F32)
        s_sums = A("s_sums", [128, NH], F32)
        s_rcp = A("s_rcp", [128, NH], F32)
        s_pre = A("s_pre", [128, 1024], BF16)
        s_prer = A("s_prer", [128, 1024], F32R)
        s_xc = A("s_xc", [128, 6 * D], F32)
        s_xnt = {p: A("s_xnt_" + p, [128, D], BF16) for p in ("hi", "lo")}
        s_xn = A("s_xn", [128, 8 * 512], F32R)
        s_q = A("s_q", [128, 8 * GRP], F32R)
        s_k = A("s_k", [128, 8 * 384], F32R)
        s_v = {p: A("s_v_" + p, [128, 3 * D], BF16) for p in ("hi", "lo")}
        s_O = A("s_O", [128, 8 * GRP], F32R)
        s_P0 = A("s_P0", [128, 2 * 256], F32)
        s_Ph = A("s_Ph", [128, 2 * 256], BF16)
        s_Pl = A("s_Pl", [128, 2 * 256], BF16)
        s_PT = A("s_PT", [128, 2 * 512], BF16)
        s_wbuf = [A("s_wbuf%d" % b, [128, 512], F32R) for b in range(4)]
        s_d0b = [A("s_d0b%d" % b, [128, 512], F32R) for b in range(6)]
        s_k1b = [A("s_k1b%d" % b, [128, 512], F32R) for b in range(6)]
        s_w2b = [A("s_w2b%d" % b, [128, D], F32R) for b in range(2)]
        s_a0 = A("s_a0", [128, 32 * 512], F32R)
        s_a1 = A("s_a1", [128, 2 * 512], F32R)

        ps = [nc.alloc_psum_tensor("ps%d" % i, [128, 512], F32) for i in range(6)]
        psb = [nc.alloc_psum_tensor("pb%d" % i, [128, 1024], BF16) for i in range(2)]

        pr = Prog(nc)
        pr.make_sems(estack)

        def ts(eng, out_ap, in_ap, s1, s2, op0, op1, reads, writes):
            def emit(e, o=out_ap, i=in_ap, s1=s1, s2=s2, op0=op0, op1=op1):
                return e.tensor_scalar(out=o, in0=i, scalar1=s1, scalar2=s2,
                                       op0=op0, op1=op1)
            pr.op(eng, emit, reads=reads, writes=writes)

        def stt(eng, out_ap, in0_ap, sc, in1_ap, op0, op1, reads, writes):
            def emit(e, o=out_ap, i0=in0_ap, sc=sc, i1=in1_ap, op0=op0, op1=op1):
                return e.scalar_tensor_tensor(out=o, in0=i0, scalar=sc, in1=i1,
                                              op0=op0, op1=op1)
            pr.op(eng, emit, reads=reads, writes=writes)

        def tt(eng, out_ap, a_ap, b_ap, alu, reads, writes):
            def emit(e, o=out_ap, a=a_ap, b=b_ap, alu=alu):
                return e.tensor_tensor(out=o, in0=a, in1=b, op=alu)
            pr.op(eng, emit, reads=reads, writes=writes)

        def cp(eng, out_ap, in_ap, reads, writes):
            if eng == "S":
                def emit(e, o=out_ap, i=in_ap):
                    return e.copy(out=o, in_=i)
            else:
                def emit(e, o=out_ap, i=in_ap):
                    return e.tensor_copy(out=o, in_=i)
            pr.op(eng, emit, reads=reads, writes=writes)

        def memset(eng, ap, val, reads, writes):
            def emit(e, a=ap, v=val):
                return e.memset(a, v)
            pr.op(eng, emit, reads=reads, writes=writes)

        def act(out_ap, in_ap, func, bias, scale, accum, reads, writes):
            def emit(e, o=out_ap, i=in_ap, f=func, b=bias, s=scale, a=accum):
                return e.activation(out=o, in_=i, func=f, bias=b, scale=s,
                                    accum_out=a)
            pr.op("S", emit, reads=reads, writes=writes)

        def mm(gkey, out_ap, lhsT, rhs, reads, start, stop, close, final, tp=None):
            def emit(e, o=out_ap, l=lhsT, r=rhs, st=start, sp=stop, tp=tp):
                return e.matmul(o, l, r, start=st, stop=sp, tile_position=tp)
            pr.gop("T", emit, reads=reads, gkey=gkey, first=start, close=close,
                   final=final)

        def tr(gkey, out_ap, in_ap, reads, first, close, final):
            def emit(e, o=out_ap, i=in_ap):
                return e.transpose(o, i, s_cid[:])
            pr.gop("T", emit, reads=tuple(reads) + (("cid",),), gkey=gkey,
                   first=first, close=close, final=final)

        groups = [(sq, g) for sq in range(n_seq) for g in range(n_grp)]
        assert len(groups) % 2 == 0
        npairs = len(groups) // 2

        def xnk(pos):
            return [("xn", pos, 0), ("xn", pos, 1)]
        xn_all = xnk(0) + xnk(1)
        a0keys = [("a0", m) for m in range(32)]
        r1024 = 1.0 / 1024.0

        pf_list = []
        for _pj in range(npairs):
            for _g in range(2):
                for _nm in ("wq", "wk", "wv", "wo"):
                    for _ic in range(8):
                        for _hf in range(2):
                            pf_list.append((_nm, _ic, _hf))
        pf_state = [0, 0]

        def pf_issue(n):
            for _ in range(n):
                i = pf_state[0]
                if i >= len(pf_list) or i - pf_state[1] >= 4:
                    return
                nm, ic, hf = pf_list[i]
                b = i % 4
                pr.dma("Q0", s_wbuf[b][:],
                       wt_d[nm][ic * 128:(ic + 1) * 128,
                                hf * 512:(hf + 1) * 512],
                       reads=[("wbuf", b)], writes=[("wbuf", b)])
                pf_state[0] = i + 1

        def panel_use(nm, ic, hf):
            i = pf_state[1]
            assert pf_list[i] == (nm, ic, hf), (pf_list[i], nm, ic, hf)
            assert i < pf_state[0]
            pf_state[1] = i + 1
            return i % 4

        def pre_consts():
            pr.dma("Q0", s_cid[:], cid_d, writes=[("cid",)])
            pr.dma("Q0", s_cidr[:], cidr_d, writes=[("cidr",)])
            pr.dma("Q0", s_msk[:], cmsk_d, writes=[("msk",)])
            pr.dma("Q0", s_msk0[:], cmsk0_d, writes=[("msk0",)])
            pr.dma("Q0", s_scal[:], scal_d, writes=[("scal",)])
            pr.dma("Q0", s_n12[:], cn12_d, writes=[("n12",)])
            ts("V", s_S0[:], s_scal[:, 0:1], 4.0, None, ALU.mult, ALU.bypass,
               reads=[("scal",)], writes=[("S0",)])
            ts("V", s_S1[:], s_scal[:, 1:2], 2.0, None, ALU.mult, ALU.bypass,
               reads=[("scal",)], writes=[("S1",)])
            ts("V", s_S2[:], s_scal[:, 2:3], 1.0, None, ALU.mult, ALU.bypass,
               reads=[("scal",)], writes=[("S2",)])
            pr.dma("Q0", s_k[:], czero_d, reads=[],
                   writes=[("kcar",)] + [("k", b) for b in range(4)])
            for p in ("hi", "lo"):
                memset("V", s_v[p][:, 2 * D:3 * D], 0.0, reads=[],
                       writes=[("v", p, 2, 0), ("v", p, 2, 1)])

        pre_tasks = []

        tsk_n = [0]

        def task_d0(m, qh):
            def run():
                kt = ("pre", 0)
                kb = ("pre", 1)
                kr = ("prer", tsk_n[0] % 2)
                bt = s_pre[:, 0:512]
                bb = s_pre[:, 512:1024]
                br = s_prer[:, (tsk_n[0] % 2) * 512:(tsk_n[0] % 2) * 512 + 512]
                r0 = m * 128
                pr.dma("Q1", bt, k0s_d[r0:r0 + 128, qh * 512:qh * 512 + 512],
                       reads=[kt], writes=[kt])
                pr.dma("Q1", bb, k0s_d[r0:r0 + 128,
                                       1024 + qh * 512:1024 + qh * 512 + 512],
                       reads=[kb], writes=[kb])
                tt("V", br, bt, bb, ALU.subtract, reads=[kt, kb, kr],
                   writes=[kr])
                pr.dma("Q0", d0s_d[r0:r0 + 128, qh * 512:(qh + 1) * 512], br,
                       reads=[kr], writes=[("d0sd", m, qh)])
                tsk_n[0] += 1
            return run

        def task_pm(src_d, dst_d, rc, c0, wkey):
            def run():
                h = tsk_n[0] % 2
                kk = ("pre", h)
                kr = ("prer", h)
                buf = s_pre[:, h * 512:(h + 1) * 512]
                bufr = s_prer[:, h * 512:(h + 1) * 512]
                pr.dma("Q1", buf, src_d[rc:rc + 128, c0:c0 + 512], reads=[kk],
                       writes=[kk])
                if h == 0:
                    act(bufr, buf, AF.Copy, -1.0, 2.0, None,
                        reads=[kk, kr], writes=[kr])
                else:
                    ts("V", bufr, buf, 2.0, -1.0, ALU.mult, ALU.add,
                       reads=[kk, kr], writes=[kr])
                pr.dma("Q0", dst_d[rc:rc + 128, c0:c0 + 512], bufr,
                       reads=[kr], writes=[wkey])
                tsk_n[0] += 1
            return run

        def build_pre_tasks():
            for m in range(32):
                for qh in range(2):
                    pre_tasks.append(task_d0(m, qh))
            for oc in range(32):
                for hf2 in range(2):
                    pre_tasks.append(task_pm(k2_d, w2pm_d, oc * 128,
                                             hf2 * 512, ("w2pmd", oc, hf2)))
            for m2 in range(32):
                for e in range(8):
                    pre_tasks.append(task_pm(k1s_d, w1pm_d, m2 * 128,
                                             e * 512, ("w1pmd", m2, e)))

        def drain(n):
            for _ in range(n):
                if pre_tasks:
                    pre_tasks.pop(0)()

        def phase_X_ln(gi, cc):
            sq, g = groups[gi]
            gch = 2 * gi + cc
            slot = gch % 6
            xt = s_xc[:, slot * D:(slot + 1) * D]
            kx = ("xc", slot)
            r0 = sq * SEQ + g * GRP + cc * 128
            L = lambda i, cc=cc: ("ln", cc, i)
            lc = s_ln[:, cc * 8:(cc + 1) * 8]
            pr.dma("Q0", xt, x_d[r0:r0 + 128, :], reads=[kx], writes=[kx])
            def em_rs(e, o=lc[:, 0:1], i=xt):
                return e.reduce_sum(o, i, AX)
            pr.op("V", em_rs, reads=[kx], writes=[L(0)])
            ts("V", lc[:, 1:2], lc[:, 0:1], r1024, None, ALU.mult,
               ALU.bypass, reads=[L(0)], writes=[L(1)])
            act(s_xnt["hi"][:, 0:D], xt, AF.Square, 0.0, 1.0, lc[:, 2:3],
                reads=[kx], writes=[("xnt", "hi"), L(2)])
            ts("V", lc[:, 3:4], lc[:, 2:3], r1024, None, ALU.mult,
               ALU.bypass, reads=[L(2)], writes=[L(3)])
            tt("V", lc[:, 4:5], lc[:, 1:2], lc[:, 1:2], ALU.mult,
               reads=[L(1)], writes=[L(4)])
            tt("V", lc[:, 5:6], lc[:, 3:4], lc[:, 4:5], ALU.subtract,
               reads=[L(3), L(4)], writes=[L(5)])
            ts("V", lc[:, 5:6], lc[:, 5:6], 1.0, 1e-5, ALU.mult, ALU.add,
               reads=[L(5)], writes=[L(5)])
            def em_sq(e, o=lc[:, 6:7], i=lc[:, 5:6]):
                return e.sqrt(o, i)
            pr.op("S", em_sq, reads=[L(5)], writes=[L(6)])
            def em_rc(e, o=lc[:, 7:8], i=lc[:, 6:7]):
                return e.reciprocal(o, i)
            pr.op("V", em_rc, reads=[L(6)], writes=[L(7)])
            for hf in range(2):
                xnf = s_P0[:, 0:512]
                kf = ("P0", 0)
                xh = xt[:, hf * 512:(hf + 1) * 512]
                hi = s_xnt["hi"][:, hf * 512:(hf + 1) * 512]
                lo = s_xnt["lo"][:, hf * 512:(hf + 1) * 512]
                ts("V", xnf, xh, lc[:, 1:2], None, ALU.subtract, ALU.bypass,
                   reads=[kx, L(1), kf], writes=[kf])
                ts("V", hi, xnf, lc[:, 7:8], None, ALU.mult,
                   ALU.bypass, reads=[kf, L(7)], writes=[("xnt", "hi")])
                stt("V", lo, xnf, lc[:, 7:8], hi, ALU.mult, ALU.subtract,
                    reads=[kf, L(7), ("xnt", "hi")], writes=[("xnt", "lo")])

        def phase_X_tr(gi, cc):
            pos = gi % 2
            for ip, p in enumerate(("hi", "lo")):
                bk = ("pb", ip)
                for ic in range(8):
                    src_ = s_xnt[p][:, ic * 128:(ic + 1) * 128]
                    dst = psb[ip][:, ic * 128:(ic + 1) * 128]
                    tr(bk, dst, src_, [("xnt", p)],
                       first=(ic == 0), close=(ic == 7), final=(ic == 7))
            cp("V", s_pre[:], psb[1][:, 0:1024],
               reads=[("pb", 1)], writes=[("pre", 0), ("pre", 1)])
            dsts = bass.AP(s_xn, pos * 256 + cc * 128,
                           [[8 * 512, 128], [512, 8], [1, 128]])
            src0 = bass.AP(psb[0], 0, [[1024, 128], [128, 8], [1, 128]])
            src1 = bass.AP(s_pre, 0, [[1024, 128], [128, 8], [1, 128]])
            tt("V", dsts, src0, src1, ALU.add,
               reads=[("pb", 0), ("pb", 1), ("pre", 0), ("pre", 1)],
               writes=[("xn", pos, cc)])

        def xblock(gi, cc):
            phase_X_ln(gi, cc)
            phase_X_tr(gi, cc)



        def phase_P_qk(gi):
            pos = gi % 2
            for nm, okey in (("wq", "q"), ("wk", "k")):
                bmap = (0, 1, 2, 3) if okey == "q" else (4, 5, 0, 1)
                for ic in range(8):
                    for hf in range(2):
                        b = panel_use(nm, ic, hf)
                        for mi in range(4):
                            m = hf * 4 + mi
                            bank = bmap[m // 2]
                            bk = ("ps", bank)
                            lh = s_wbuf[b][:, mi * 128:(mi + 1) * 128]
                            rh = s_xn[:, ic * 512 + pos * 256:
                                      ic * 512 + pos * 256 + 256]
                            out = ps[bank][:, (m % 2) * 256:(m % 2 + 1) * 256]
                            rds = ([("wbuf", b)] + xnk(pos)) if mi % 2 == 0 else []
                            mm(bk, out, lh, rh, rds,
                               start=(ic == 0 and m % 2 == 0),
                               stop=(ic == 7 and m % 2 == 1),
                               close=(m % 2 == 1),
                               final=(ic == 7 and m % 2 == 1))
                        pf_issue(1)
                for pr4 in range(4):
                    bank = bmap[pr4]
                    bk = ("ps", bank)
                    if okey == "q":
                        dst = s_q[:, pr4 * 512:(pr4 + 1) * 512]
                        src = ps[bank][:, 0:512]
                    else:
                        dst = bass.AP(s_k, 2 * pr4 * 384 + 128,
                                      [[8 * 384, 128], [384, 2], [1, 256]])
                        src = bass.AP(ps[bank], 0,
                                      [[512, 128], [256, 2], [1, 256]])
                    cp("S", dst, src, reads=[bk], writes=[(okey, pr4)])

        def phase_P_v(gi):
            pos = gi % 2
            vch0 = 2 * gi
            for ic in range(8):
                for hf in range(2):
                    b = panel_use("wv", ic, hf)
                    for tc in range(2):
                        bank = tc * 2 + hf
                        lh = s_xn[:, ic * 512 + pos * 256 + tc * 128:
                                  ic * 512 + pos * 256 + (tc + 1) * 128]
                        out = ps[bank][:, 0:512]
                        mm(("ps", bank), out, lh, s_wbuf[b][:],
                           [("wbuf", b)] + xnk(pos),
                           start=(ic == 0), stop=(ic == 7), close=True,
                           final=(ic == 7))
                    pf_issue(1)
            for tc in (1, 0):
                sl = (vch0 + tc) % 3
                for hf in range(2):
                    bank = tc * 2 + hf
                    bk = ("ps", bank)
                    hi = s_v["hi"][:, sl * D + hf * 512:sl * D + (hf + 1) * 512]
                    lo = s_v["lo"][:, sl * D + hf * 512:sl * D + (hf + 1) * 512]
                    cp("S", hi, ps[bank][:, 0:512], reads=[bk],
                       writes=[("v", "hi", sl, hf)])
                    stt("V", lo, ps[bank][:, 0:512], 1.0, hi, ALU.mult,
                        ALU.subtract, reads=[bk, ("v", "hi", sl, hf)],
                        writes=[("v", "lo", sl, hf)])

        def phase_A(gi):
            sq, g = groups[gi]
            def body(cc):
                vch = 2 * gi + cc
                seq_start = (g == 0 and cc == 0)
                msk_s = s_msk0 if seq_start else s_msk
                mkey = ("msk0",) if seq_start else ("msk",)

                def vap(p, h, blk, vch=vch):
                    sl = (vch - 1 + blk) % 3
                    c0 = h * HD
                    return (s_v[p][:, sl * D + c0:sl * D + c0 + HD],
                            ("v", p, sl, c0 // 512))

                def s_wave(w, cc=cc, msk_s=msk_s, mkey=mkey):
                    sbank = 2 + (w % 4)
                    sbk = ("ps", sbank)
                    for hh in range(2):
                        h = 2 * w + hh
                        pb = 64 * (h % 2)
                        m = h // 2
                        tp = (pb, 0) if pb else None
                        qh = s_q[pb:pb + 64, m * 256 + cc * 128:
                                 m * 256 + (cc + 1) * 128]
                        kh = s_k[pb:pb + 64, m * 384 + cc * 128:
                                 m * 384 + cc * 128 + 256]
                        omsk = ps[sbank][:, hh * 256:hh * 256 + 256]
                        rds = [("q", m // 2), ("k", m // 2)]
                        if cc == 0:
                            rds.append(("kcar",))
                        mm(sbk, omsk, qh, kh, rds, start=(hh == 0),
                           stop=False, close=False, final=False, tp=tp)
                        mm(sbk, omsk, s_cidr[:], msk_s[:],
                           [mkey, ("cidr",)], start=False, stop=(hh == 1),
                           close=True, final=(hh == 1))

                def head_work(w, cc=cc, vap=vap):
                    sbank = 2 + (w % 4)
                    sbk = ("ps", sbank)
                    for hh in range(2):
                        h = 2 * w + hh
                        slot = h % 2
                        act(s_P0[:, slot * 256:(slot + 1) * 256],
                            ps[sbank][:, hh * 256:hh * 256 + 256],
                            AF.Exp, s_n12[:, 0:1], 0.125,
                            s_sums[:, h:h + 1],
                            reads=[sbk, ("n12",)],
                            writes=[("P0", slot), ("sums", h)])
                    def em_rcp(e, o=s_rcp[:, 2 * w:2 * w + 2],
                               i=s_sums[:, 2 * w:2 * w + 2]):
                        return e.reciprocal(o, i)
                    pr.op("V", em_rcp,
                          reads=[("sums", 2 * w), ("sums", 2 * w + 1)],
                          writes=[("rcp", w)])
                    for hh in range(2):
                        h = 2 * w + hh
                        slot = h % 2
                        p0 = s_P0[:, slot * 256:(slot + 1) * 256]
                        ph = s_Ph[:, slot * 256:(slot + 1) * 256]
                        pl = s_Pl[:, slot * 256:(slot + 1) * 256]
                        rc = s_rcp[:, h:h + 1]
                        act(ph, p0, AF.Copy, 0.0, rc,
                            None, reads=[("P0", slot), ("rcp", w)],
                            writes=[("Ph", slot)])
                        stt("V", pl, p0, rc, ph, ALU.mult, ALU.subtract,
                            reads=[("P0", slot), ("rcp", w), ("Ph", slot)],
                            writes=[("Pl", slot)])
                    for hh in range(2):
                        h = 2 * w + hh
                        slot = h % 2
                        ptk = ("pb", slot)
                        for idx in range(4):
                            ti, blk = divmod(idx, 2)
                            sap = (s_Ph if ti == 0 else s_Pl)[
                                :, slot * 256 + blk * 128:
                                slot * 256 + (blk + 1) * 128]
                            dst = psb[slot][:, idx * 128:(idx + 1) * 128]
                            tr(ptk, dst, sap,
                               [("Ph", slot), ("Pl", slot)],
                               first=(idx == 0), close=(idx == 3),
                               final=(idx == 3))
                    for hh in range(2):
                        h = 2 * w + hh
                        slot = h % 2
                        ptk = ("pb", slot)
                        dpt = s_PT[:, slot * 512:(slot + 1) * 512]
                        cp("V", dpt, psb[slot][:, 0:512], reads=[ptk],
                           writes=[("PT", slot)])
                    for hh in range(2):
                        h = 2 * w + hh
                        slot = h % 2
                        obank = w // 4
                        obk = ("ps", obank)
                        pb = 64 * (h % 2)
                        qq = w % 4
                        oout = ps[obank][pb:pb + 64, qq * 128:(qq + 1) * 128]
                        otp = (0, pb) if pb else None
                        seqm = []
                        for blk in range(2):
                            vh, vkh = vap("hi", h, blk)
                            vl, vkl = vap("lo", h, blk)
                            pthi = s_PT[:, slot * 512 + blk * 128:
                                        slot * 512 + (blk + 1) * 128]
                            ptlo = s_PT[:, slot * 512 + 256 + blk * 128:
                                        slot * 512 + 256 + (blk + 1) * 128]
                            seqm.append((vh, pthi, [vkh, ("PT", slot)]))
                            seqm.append((vh, ptlo, []))
                            seqm.append((vl, pthi, [vkl]))
                        for si, (lh, rr, rds) in enumerate(seqm):
                            mm(obk, oout, lh, rr, rds,
                               start=(si == 0), stop=(si == 5),
                               close=(si == 5),
                               final=(w % 4 == 3 and hh == 1 and si == 5),
                               tp=otp)

                def oevac(cc=cc):
                    for ob in range(2):
                        obk = ("ps", ob)
                        dsts = bass.AP(s_O, ob * 4 * GRP + cc * 128,
                                       [[8 * GRP, 128], [GRP, 4], [1, 128]])
                        srcs = bass.AP(ps[ob], 0,
                                       [[512, 128], [128, 4], [1, 128]])
                        cp("S", dsts, srcs, reads=[obk],
                           writes=[("O", ob, cc)])
                return s_wave, head_work, oevac

            sw0, hw0, oe0 = body(0)
            sw1, hw1, oe1 = body(1)
            waves = [(sw0, hw0)] * 8 + [(sw1, hw1)] * 8
            for W in range(3):
                waves[W][0](W % 8)
            for W in range(16):
                if W + 3 < 16:
                    waves[W + 3][0]((W + 3) % 8)
                waves[W][1](W % 8)
                if W == 7:
                    oe0()
            oe1()
            dstc = bass.AP(s_k, 0, [[8 * 384, 128], [384, 8], [1, 128]])
            srcc = bass.AP(s_k, 256, [[8 * 384, 128], [384, 8], [1, 128]])
            pr.dma("Q1", dstc, srcc,
                   reads=[("k", b) for b in range(4)] + [("kcar",)],
                   writes=[("kcar",)])

        def phase_Wo(gi):
            gi2 = 2 * gi
            for ic in range(8):
                for hf in range(2):
                    b = panel_use("wo", ic, hf)
                    for tc in range(2):
                        bank = tc * 2 + hf
                        lh = s_O[:, ic * 256 + tc * 128:ic * 256 + (tc + 1) * 128]
                        mm(("ps", bank), ps[bank][:, 0:512], lh, s_wbuf[b][:],
                           [("wbuf", b), ("O", ic // 4, tc)],
                           start=(ic == 0), stop=(ic == 7), close=True,
                           final=(ic == 7))
                    pf_issue(1)
            for tc in range(2):
                slot = (gi2 + tc) % 6
                kx = ("xc", slot)
                for hf in range(2):
                    bank = tc * 2 + hf
                    bk = ("ps", bank)
                    xap = s_xc[:, slot * D + hf * 512:slot * D + (hf + 1) * 512]
                    tt("V", xap, ps[bank][:, 0:512], xap, ALU.add,
                       reads=[bk, kx], writes=[kx])

        def phase_F0(pj):
            def strip_dma(idx):
                if idx >= 64:
                    return
                m, h = divmod(idx, 2)
                b = idx % 6
                assert ("d0sd", m, h) in pr.bufs
                pr.dma("Q0" if idx % 2 == 0 else "Q1", s_d0b[b][:],
                       d0s_d[m * 128:(m + 1) * 128, h * 512:(h + 1) * 512],
                       reads=[("d0b", b), ("d0sd", m, h)],
                       writes=[("d0b", b)])
            drain(5)
            for idx in range(5):
                strip_dma(idx)
            for m in range(32):
                drain(2)
                pf_issue(1)
                bank = 4 + (m % 2)
                bk = ("ps", bank)
                out = ps[bank][:, 0:512]
                for ic in range(8):
                    h = ic // 4
                    b = (m * 2 + h) % 6
                    lh = s_d0b[b][:, (ic % 4) * 128:(ic % 4 + 1) * 128]
                    rh = s_xn[:, ic * 512:(ic + 1) * 512]
                    rds = [("d0b", b)] + (xn_all if ic == 0 else [])
                    mm(bk, out, lh, rh, rds, start=(ic == 0), stop=(ic == 7),
                       close=(ic % 4 == 3), final=(ic == 7))
                    if ic % 4 == 3:
                        strip_dma(m * 2 + h + 5)
                act(s_a0[:, m * 512:(m + 1) * 512], out, AF.Relu, 0.0,
                    s_S0[:, 0:1], None, reads=[bk, ("S0",)],
                    writes=[("a0", m)])

        def g1_mms(m2):
            def strip_dma(idx):
                if idx >= 256:
                    return
                m2_, e_ = divmod(idx, 8)
                b = idx % 6
                assert ("w1pmd", m2_, e_) in pr.bufs
                pr.dma("Q0" if idx % 2 == 0 else "Q1", s_k1b[b][:],
                       w1pm_d[m2_ * 128:(m2_ + 1) * 128,
                              e_ * 512:(e_ + 1) * 512],
                       reads=[("k1b", b), ("w1pmd", m2_, e_)],
                       writes=[("k1b", b)])
            drain(8)
            pf_issue(1)
            if m2 == 0:
                for idx in range(5):
                    strip_dma(idx)
            sl = m2 % 2
            assert ("w2pmd", m2, 1) in pr.bufs
            pr.dma("Q0" if m2 % 2 == 0 else "Q1", s_w2b[sl][:],
                   w2pm_d[m2 * 128:(m2 + 1) * 128, :],
                   reads=[("w2b", sl), ("w2pmd", m2, 0), ("w2pmd", m2, 1)],
                   writes=[("w2b", sl)])
            bank = m2 % 2
            bk = ("ps", bank)
            out = ps[bank][:, 0:512]
            for kc in range(32):
                e = kc // 4
                b = (m2 * 8 + e) % 6
                lh = s_k1b[b][:, (kc % 4) * 128:(kc % 4 + 1) * 128]
                rh = s_a0[:, kc * 512:(kc + 1) * 512]
                rds = [("k1b", b)] + (a0keys if kc == 0 else [])
                mm(bk, out, lh, rh, rds, start=(kc == 0), stop=(kc == 31),
                   close=(kc % 4 == 3), final=(kc == 31))
                if kc % 4 == 3:
                    strip_dma(m2 * 8 + e + 5)
            act(s_a1[:, sl * 512:(sl + 1) * 512], out, AF.Relu, 0.0,
                s_S1[:, 0:1], None, reads=[bk, ("S1",)], writes=[("a1", sl)])

        def g2_batch(pj, bt):
            batch = range(2 * bt, 2 * bt + 2)
            akeys = [("a1", m2 % 2) for m2 in batch]
            wkeys = [("w2b", m2 % 2) for m2 in batch]
            for ti in range(8):
                tc, oh = divmod(ti, 2)
                bank = 2 + (ti % 4)
                gk = ("ps", bank)
                for j, m2 in enumerate(batch):
                    sl = m2 % 2
                    mm(gk, ps[bank][:, 0:512],
                       s_a1[:, sl * 512 + tc * 128:sl * 512 + (tc + 1) * 128],
                       s_w2b[sl][:, oh * 512:(oh + 1) * 512],
                       (akeys + wkeys) if j == 0 else [],
                       start=(j == 0), stop=(j == 1), close=(j == 1),
                       final=True)
                slot = (4 * pj + tc) % 6
                xap = s_xc[:, slot * D + oh * 512:slot * D + (oh + 1) * 512]
                stt("V", xap, ps[bank][:, 0:512], s_S2[:, 0:1], xap,
                    ALU.mult, ALU.add, reads=[gk, ("S2",), ("xc", slot)],
                    writes=[("xc", slot)])

        def phase_F12(pj):
            gi0 = 2 * pj
            nxtA = gi0 + 2 if gi0 + 2 < len(groups) else None

            drain(40)

            def ystore(gi, cc):
                sq, g = groups[gi]
                slot = (2 * gi + cc) % 6
                r0 = sq * SEQ + g * GRP + cc * 128
                pr.dma("Q1", y_d[r0:r0 + 128, :],
                       s_xc[:, slot * D:(slot + 1) * D],
                       reads=[("xc", slot)], writes=[("ydone", gi, cc)])

            for bt in range(16):
                for m2 in range(2 * bt, 2 * bt + 2):
                    g1_mms(m2)
                g2_batch(pj, bt)
                if nxtA is not None:
                    if bt == 2:
                        phase_X_ln(nxtA, 0)
                    elif bt == 4:
                        phase_X_tr(nxtA, 0)
                    elif bt == 6:
                        phase_X_ln(nxtA, 1)
                    elif bt == 8:
                        phase_X_tr(nxtA, 1)
            for pos in range(2):
                for cc in range(2):
                    ystore(gi0 + pos, cc)

        pre_consts()
        build_pre_tasks()
        for cc in range(2):
            xblock(0, cc)
        pf_issue(4)
        for pj in range(npairs):
            gia, gib = 2 * pj, 2 * pj + 1
            phase_P_qk(gia)
            drain(18)
            phase_P_v(gia)
            drain(18)
            phase_A(gia)
            drain(18)
            phase_X_ln(gib, 0)
            phase_X_tr(gib, 0)
            phase_X_ln(gib, 1)
            phase_X_tr(gib, 1)
            drain(18)
            phase_Wo(gia)
            drain(18)
            phase_P_qk(gib)
            drain(18)
            phase_P_v(gib)
            drain(18)
            phase_A(gib)
            drain(18)
            phase_Wo(gib)
            drain(18)
            phase_F0(pj)
            phase_F12(pj)
        assert not pre_tasks, len(pre_tasks)

        pr.finalize_queues()
        pr.render()
    return nc




_CACHE = {}


def _consts():
    i = np.arange(128)[:, None]
    j = np.arange(256)[None, :]
    band = (j >= i + 1) & (j <= i + 128)
    valid0 = band & (j >= 128)
    cmsk = np.where(band, 0.0, MASK_RAW).astype(np.float32)
    cmsk0 = np.where(valid0, 0.0, MASK_RAW).astype(np.float32)
    cid = np.eye(128, dtype=ml_dtypes.bfloat16)
    cidr = np.eye(128, dtype=np.float32)
    cn12 = np.full((128, 1), EXP_SHIFT, dtype=np.float32)
    return cmsk, cmsk0, cid, cidr, cn12


def make_inputs(L0_attn_x, L0_attn_Wq, L0_attn_Wk, L0_attn_Wv, L0_attn_Wo,
                L0_ffn_k0, L0_ffn_k1, L0_ffn_k2, L0_ffn_s0, L0_ffn_s1,
                L0_ffn_s2):
    x = np.ascontiguousarray(np.asarray(L0_attn_x, dtype=np.float32))
    wqt = np.ascontiguousarray(np.asarray(L0_attn_Wq, np.float32).T)
    wkt = np.ascontiguousarray(np.asarray(L0_attn_Wk, np.float32).T)
    wvt = np.ascontiguousarray(np.asarray(L0_attn_Wv, np.float32).T)
    wot = np.ascontiguousarray(np.asarray(L0_attn_Wo, np.float32).T)
    k0 = np.asarray(L0_ffn_k0).astype(np.uint8)
    k0s = np.ascontiguousarray(
        k0.reshape(16, 128, 32, 128).transpose(2, 1, 0, 3)).reshape(HID, 2 * D)
    k1 = np.asarray(L0_ffn_k1).astype(np.uint8)
    k1s = np.ascontiguousarray(
        k1.reshape(32, 128, 32, 128).transpose(2, 1, 0, 3)).reshape(HID, HID)
    k2 = np.ascontiguousarray(np.asarray(L0_ffn_k2).astype(np.uint8))
    scal = np.zeros((128, 4), np.float32)
    scal[:, 0] = np.float32(L0_ffn_s0)
    scal[:, 1] = np.float32(L0_ffn_s1)
    scal[:, 2] = np.float32(L0_ffn_s2)
    cmsk, cmsk0, cid, cidr, cn12 = _consts()
    czero = np.zeros((128, 8 * 384), np.float32)
    shared = dict(wqt=wqt, wkt=wkt, wvt=wvt, wot=wot, k0s=k0s, k1s=k1s, k2=k2,
                  scal=scal, cmsk=cmsk, cmsk0=cmsk0, cid=cid, cidr=cidr,
                  cn12=cn12, czero=czero)
    in_maps = []
    for c in range(8):
        m = dict(shared)
        m["x"] = np.ascontiguousarray(x[2 * c:2 * c + 2].reshape(2 * SEQ, D))
        in_maps.append(m)
    return in_maps


def kernel(L0_attn_x, L0_attn_Wq, L0_attn_Wk, L0_attn_Wv, L0_attn_Wo,
           L0_ffn_k0, L0_ffn_k1, L0_ffn_k2, L0_ffn_s0, L0_ffn_s1, L0_ffn_s2):
    in_maps = make_inputs(L0_attn_x, L0_attn_Wq, L0_attn_Wk, L0_attn_Wv,
                          L0_attn_Wo, L0_ffn_k0, L0_ffn_k1, L0_ffn_k2,
                          L0_ffn_s0, L0_ffn_s1, L0_ffn_s2)
    if "nc" not in _CACHE:
        _CACHE["nc"] = build_program()
    nc = _CACHE["nc"]
    res = run_bass_kernel_spmd(nc, in_maps, core_ids=list(range(8)))
    out = np.empty((16, SEQ, D), np.float32)
    for c in range(8):
        out[2 * c:2 * c + 2] = np.asarray(res.results[c]["y"]).reshape(2, SEQ, D)
    return out
